# version 21
# baseline (speedup 1.0000x reference)
import contextlib
import numpy as np
import ml_dtypes
import concourse.bass as bass
import concourse.mybir as mybir
from concourse.alu_op_type import AluOpType as A
from concourse.bass_isa import ReduceOp
import concourse.library_config as libcfg
from concourse.bass_utils import run_bass_kernel_spmd

F32 = mybir.dt.float32
BF16 = mybir.dt.bfloat16
I32 = mybir.dt.int32
AF = mybir.ActivationFunctionType
AX = mybir.AxisListType.X

B = 8
L, D, NL, DST, DIN, NH = 4096, 1024, 4, 64, 2048, 32
CONV_DIM = DIN + 2 * DST
EIN = 2 * DIN + 2 * DST + NH
HID = 4096
EPS = 1e-5
OT, TT, Q = 1024, 512, 128
NO, NI, NQ = L // OT, OT // TT, TT // Q
ZC, XC = DIN // 128, CONV_DIM // 128
EC = 34
MAGIC = 12582912.0
CAP = 1e30
CIW = TT + 3

_DBG = None


class Tk:
    __slots__ = ("h", "w", "r", "name")

    def __init__(self, h, name=""):
        self.h, self.name, self.w, self.r = h, name, None, {}


NONE = Tk(None)


class Prog:
    def __init__(self, nc):
        self.nc = nc
        self.es = contextlib.ExitStack()
        self.eng = {"t": nc.tensor, "v": nc.vector, "s": nc.scalar,
                    "g": nc.gpsimd, "y": nc.sync}
        self.sem, self.pool, self.pool_i, self.pool_cnt = {}, {}, {}, {}
        self.cnt = {e: 0 for e in "tvsgy"}
        self.mark = {e: {} for e in "tvsgy"}
        self.drained = {e: 0 for e in "tvsgy"}
        self.dirty = {"g": set(), "y": set()}

    def setup(self, npool=48):
        for e in "tvsg":
            self.sem[e] = self.es.enter_context(self.nc.semaphore(f"p_{e}"))
        for qn in ("g", "y"):
            self.pool[qn] = [self.es.enter_context(self.nc.semaphore(f"d{qn}{i}"))
                             for i in range(npool)]
            self.pool_i[qn] = 0
            self.pool_cnt[qn] = [0] * npool

    def sb(self, name, shape, dt):
        return Tk(self.es.enter_context(self.nc.sbuf_tensor(name, shape, dt)), name)

    def ps(self, name, shape, dt):
        return Tk(self.es.enter_context(self.nc.psum_tensor(name, shape, dt)), name)

    def region(self, tk, name):
        return Tk(tk.h, name)

    def snapshot(self):
        toks = {}
        for e in "tvsg":
            if self.cnt[e]:
                toks[("snap", e)] = ("c", e, self.cnt[e])
        for qn in ("g", "y"):
            for idx in self.dirty[qn]:
                toks[("snapd", qn, idx)] = ("d", qn, idx, self.pool_cnt[qn][idx])
            self.dirty[qn] = set()
        return toks

    def _wait(self, e, tok):
        if tok[0] == "c":
            _, oe, v = tok
            if oe == e:
                if v > self.drained[e]:
                    self.eng[e].drain()
                    self.drained[e] = self.cnt[e]
            else:
                if self.mark[e].get(oe, 0) < v:
                    self.eng[e].wait_ge(self.sem[oe], v)
                    self.mark[e][oe] = v
        else:
            _, qn, idx, v = tok
            key = (qn, idx)
            if self.mark[e].get(key, 0) < v:
                self.eng[e].wait_ge(self.pool[qn][idx], v)
                self.mark[e][key] = v

    def dep(self, e, ins, outs):
        for t in ins:
            if t is not None and t.w is not None:
                self._wait(e, t.w)
        for t in outs:
            if t is None:
                continue
            if t.w is not None:
                self._wait(e, t.w)
            for tok in list(t.r.values()):
                self._wait(e, tok)

    warm_ap = None
    warm_every = 0

    def fin(self, e, instr, ins, outs):
        if e == "v" and self.warm_every and self.cnt["v"] % self.warm_every == 0:
            if self.mark["t"].get("v", 0) < self.cnt["v"]:
                self.eng["t"].wait_ge(self.sem["v"], self.cnt["v"])
                self.mark["t"]["v"] = self.cnt["v"]
            self.eng["t"].ldweights(self.warm_ap)
        self.cnt[e] += 1
        tok = ("c", e, self.cnt[e])
        instr.then_inc(self.sem[e], 1)
        for t in ins:
            if t is not None:
                t.r[("c", e)] = tok
        for t in outs:
            if t is not None:
                t.w, t.r = tok, {}
        return tok

    def tt(self, e, oap, a0, a1, op, ins, outs):
        self.dep(e, ins, outs)
        return self.fin(e, self.eng[e].tensor_tensor(out=oap, in0=a0, in1=a1, op=op),
                        ins, outs)

    def ts(self, e, oap, a0, s1, s2, op0, op1, ins, outs):
        self.dep(e, ins, outs)
        kw = dict(out=oap, in0=a0, scalar1=s1, scalar2=s2, op0=op0)
        if op1 is not None:
            kw["op1"] = op1
        return self.fin(e, self.eng[e].tensor_scalar(**kw), ins, outs)

    def stt(self, e, oap, a0, scalar, a1, op0, op1, ins, outs):
        self.dep(e, ins, outs)
        return self.fin(e, self.eng[e].scalar_tensor_tensor(
            out=oap, in0=a0, scalar=scalar, in1=a1, op0=op0, op1=op1), ins, outs)

    def cp(self, e, oap, a0, ins, outs):
        self.dep(e, ins, outs)
        if e == "s":
            return self.fin(e, self.eng[e].copy(out=oap, in_=a0), ins, outs)
        return self.fin(e, self.eng[e].tensor_copy(out=oap, in_=a0), ins, outs)

    def act(self, e, oap, a0, func, ins, outs, bias=0.0, scale=1.0):
        self.dep(e, ins, outs)
        if isinstance(bias, float):
            bias = np.float32(bias)
        if isinstance(scale, float):
            scale = np.float32(scale)
        return self.fin(e, self.eng[e].activation(out=oap, in_=a0, func=func,
                                                  bias=bias, scale=scale), ins, outs)

    def red(self, e, oap, a0, op, ins, outs, absval=False):
        self.dep(e, ins, outs)
        f = self.eng[e].reduce_max if op == "max" else self.eng[e].reduce_sum
        return self.fin(e, f(out=oap, in_=a0, axis=AX, apply_absolute_value=absval),
                        ins, outs)

    def recip(self, e, oap, a0, ins, outs):
        self.dep(e, ins, outs)
        return self.fin(e, self.eng[e].reciprocal(out=oap, in_=a0), ins, outs)

    def scan(self, e, oap, d0, d1, initial, op0, op1, ins, outs):
        self.dep(e, ins, outs)
        return self.fin(e, self.eng[e].tensor_tensor_scan(
            out=oap, data0=d0, data1=d1, initial=initial, op0=op0, op1=op1), ins, outs)

    def memset(self, e, oap, val, outs):
        self.dep(e, [], outs)
        return self.fin(e, self.eng[e].memset(oap, val), [], outs)

    def mmg(self, mms, ins, outs):
        self.dep("t", ins, outs)
        last = None
        for m in mms:
            oap, lap, rap, st, sp = m[:5]
            tp = m[5] if len(m) > 5 else None
            last = self.eng["t"].matmul(oap, lap, rap, start=st, stop=sp,
                                        tile_position=tp)
        return self.fin("t", last, ins, outs)

    def trg(self, trs, ins, outs):
        self.dep("t", ins, outs)
        last = None
        for (oap, iap, ident) in trs:
            last = self.eng["t"].transpose(oap, iap, ident)
        return self.fin("t", last, ins, outs)

    def dma(self, qn, oap, a0, ins, outs):
        self.dep(qn, ins, outs)
        idx = self.pool_i[qn]
        prev = self.pool_cnt[qn][idx]
        if prev > 0 and self.mark[qn].get((qn, idx), 0) < prev:
            self.eng[qn].wait_ge(self.pool[qn][idx], prev)
            self.mark[qn][(qn, idx)] = prev
        i = self.eng[qn].dma_start(out=oap, in_=a0)
        self.pool_i[qn] = (idx + 1) % len(self.pool[qn])
        self.pool_cnt[qn][idx] += 16
        self.dirty[qn].add(idx)
        tok = ("d", qn, idx, self.pool_cnt[qn][idx])
        i.then_inc(self.pool[qn][idx], 16)
        for t in ins:
            if t is not None:
                t.r[("q", qn, idx)] = tok
        for t in outs:
            if t is not None:
                t.w, t.r = tok, {}
        return tok


class Carver:

    def __init__(self, p, cols_bf16):
        self.p = p
        self.t = p.es.enter_context(p.nc.sbuf_tensor("arena", [128, cols_bf16], BF16))
        self.cols = cols_bf16
        self.off = 0
        self.frontier = {}

    def newgen(self):
        self.off = 0
        self.frontier = self.p.snapshot()

    def get(self, name, shape, dt):
        assert shape[0] <= 128
        esz = {F32: 4, BF16: 2, I32: 4}[dt]
        ncols = shape[1]
        nb = ncols * esz
        nb = (nb + 127) // 128 * 128
        start_b = self.off
        self.off += nb
        assert self.off <= self.cols * 2, f"arena overflow at {name}: {self.off}"
        base = self.t[:].bitcast(dt) if dt != BF16 else self.t[:]
        view = base[0:shape[0], start_b // esz: start_b // esz + ncols]
        tk = Tk(view, name)
        tk.r = dict(self.frontier)
        return tk


def svw(h, base, stride, n, w):
    return h[:, base:base + n * stride].rearrange("p (n s) -> p n s", n=n)[:, :, 0:w]


def bcv(ap2d, n):
    return ap2d.unsqueeze(1).broadcast_to([ap2d.shape[0], n, ap2d.shape[1]])


def build():
    nc = bass.Bass("TRN2", target_bir_lowering=False)
    p = Prog(nc)

    x_h = nc.dram_tensor("x", [L, D], F32, kind="ExternalInput").ap()
    out_h = nc.dram_tensor("out", [L, D], F32, kind="ExternalOutput").ap()
    in_w_h = nc.dram_tensor("in_w", [NL, D, EIN], F32, kind="ExternalInput").ap()
    out_w_h = nc.dram_tensor("out_w", [NL, DIN, D], F32, kind="ExternalInput").ap()
    convw_h = nc.dram_tensor("conv_w", [NL, 128, XC * 4], F32, kind="ExternalInput").ap()
    convb_h = nc.dram_tensor("conv_b", [NL, 128, XC], F32, kind="ExternalInput").ap()
    lnw_h = nc.dram_tensor("ln_w", [NL, 128, 8], F32, kind="ExternalInput").ap()
    lnb_h = nc.dram_tensor("ln_b", [NL, 128, 8], F32, kind="ExternalInput").ap()
    dtb_h = nc.dram_tensor("dt_bias", [NL, NH], F32, kind="ExternalInput").ap()
    alog_h = nc.dram_tensor("A_log", [NL, NH], F32, kind="ExternalInput").ap()
    dp_h = nc.dram_tensor("Dp", [NL, 1, NH], F32, kind="ExternalInput").ap()
    nw_h = nc.dram_tensor("norm_w", [NL, 128, ZC], F32, kind="ExternalInput").ap()
    w1_h = nc.dram_tensor("w1", [HID, D], F32, kind="ExternalInput").ap()
    b1_h = nc.dram_tensor("b1", [128, 32], F32, kind="ExternalInput").ap()
    w2_h = nc.dram_tensor("w2", [D, HID], F32, kind="ExternalInput").ap()
    b2_h = nc.dram_tensor("b2", [1, D], F32, kind="ExternalInput").ap()
    ln1w_h = nc.dram_tensor("ln1_w", [128, 8], F32, kind="ExternalInput").ap()
    ln1b_h = nc.dram_tensor("ln1_b", [128, 8], F32, kind="ExternalInput").ap()
    ln2w_h = nc.dram_tensor("ln2_w", [128, 32], F32, kind="ExternalInput").ap()
    ln2b_h = nc.dram_tensor("ln2_b", [128, 32], F32, kind="ExternalInput").ap()
    ident_h = nc.dram_tensor("c_ident", [128, 128], BF16, kind="ExternalInput").ap()
    mask_h = nc.dram_tensor("c_mask", [128, 128], BF16, kind="ExternalInput").ap()
    sel_h = nc.dram_tensor("c_sel", [64, NH * Q], BF16, kind="ExternalInput").ap()
    mneg_h = nc.dram_tensor("c_mneg", [128, 512], BF16, kind="ExternalInput").ap()
    orow_h = nc.dram_tensor("c_orow", [1, 128], BF16, kind="ExternalInput").ap()
    ocol_h = nc.dram_tensor("c_ocol", [128, 1], BF16, kind="ExternalInput").ap()
    oc32_h = nc.dram_tensor("c_oc32", [128, 1], F32, kind="ExternalInput").ap()
    or32_h = nc.dram_tensor("c_or32", [1, 128], F32, kind="ExternalInput").ap()
    o32_h = nc.dram_tensor("c_o32", [NH, Q], F32, kind="ExternalInput").ap()
    qw1_s = nc.dram_tensor("qw1_s", [D, HID], BF16, kind="Internal").ap()
    qw2_s = nc.dram_tensor("qw2_s", [HID, D], BF16, kind="Internal").ap()
    inw_s = nc.dram_tensor("inw_s", [NL, D, EIN], BF16, kind="Internal").ap()
    outw_s = nc.dram_tensor("outw_s", [NL, DIN, D], BF16, kind="Internal").ap()

    dbg = {}
    if _DBG:
        for name, shape, dt in _DBG:
            dbg[name] = nc.dram_tensor("dbg_" + name, shape, dt,
                                       kind="ExternalOutput").ap()

    p.setup()
    sb, ps, rg = p.sb, p.ps, p.region

    xseg = sb("xseg", [128, 8 * OT + OT], BF16)
    wbig = sb("wbig", [128, 8 * 2208], BF16)
    wbA = Tk(wbig.h[:, 0:4 * 2208], "wbA")
    wbB = Tk(wbig.h[:, 4 * 2208:8 * 2208], "wbB")
    hst = [sb(f"hst{l}", [64, DIN], BF16) for l in range(NL)]
    hro = [0 for l in range(NL)]
    halo = [sb(f"halo{l}", [128, XC * 3], BF16) for l in range(NL)]
    convw = [sb(f"convw{l}", [128, XC * 4], F32) for l in range(NL)]
    convbb = [sb(f"convb{l}", [128, XC], F32) for l in range(NL)]
    lnw = [sb(f"lnw{l}", [128, 8], F32) for l in range(NL)]
    lnb = [sb(f"lnb{l}", [128, 8], F32) for l in range(NL)]
    dtb = [sb(f"dtb{l}", [NH, 1], F32) for l in range(NL)]
    nA = [sb(f"nA{l}", [NH, 1], F32) for l in range(NL)]
    dprow = [sb(f"dprow{l}", [1, NH], BF16) for l in range(NL)]
    nwv = [sb(f"nwv{l}", [128, ZC], F32) for l in range(NL)]
    stage = sb("stage", [128, D], F32)
    stg16 = sb("stg16", [128, D], BF16)
    sq = sb("sq", [128, TT], BF16)
    sq2 = sb("sq2", [128, TT], BF16)
    srow = sb("srow", [1, TT], F32)
    mrow = sb("mrow", [1, TT], F32)
    rrow = sb("rrow", [1, TT], F32)
    brow = sb("brow", [1, 2 * TT], BF16)
    bcast = sb("bcast", [128, 2 * TT], BF16)
    mu128 = sb("mu128", [128, 1], F32)
    sd128 = sb("sd128", [128, 1], F32)
    v128 = sb("v128", [128, 1], F32)
    wqs = sb("wqs", [128, 1], F32)
    wqr = sb("wqr", [1, 1], F32)
    wiv = [sb(f"wiv{i}", [1, 1], F32) for i in range(2)]
    wq16 = sb("wq16", [128, 1], BF16)
    b1c = sb("b1c", [128, 32], F32)
    b2row = sb("b2row", [1, D], BF16)
    b2bc = sb("b2bc", [128, D], BF16)
    invc = sb("invc", [128, 4], F32)
    g1c = sb("g1c", [128, 8], F32)
    e1c = sb("e1c", [128, 8], F32)
    g2c = sb("g2c", [128, 32], F32)
    e2c = sb("e2c", [128, 32], F32)
    ident = sb("ident", [128, 128], BF16)
    mask01 = sb("mask01", [128, 128], BF16)
    mneg = sb("mneg", [128, 512], BF16)
    orow = sb("orow", [1, 128], BF16)
    ocol = sb("ocol", [128, 1], BF16)
    oc32 = sb("oc32", [128, 1], F32)
    or32 = sb("or32", [1, 128], F32)
    wqr128 = sb("wqr128", [128, 1], F32)
    wiv128 = [sb(f"wiv128_{i}", [128, 1], F32) for i in range(2)]
    mq4 = sb("mq4", [128, 4], F32)
    mt4 = sb("mt4", [128, 4], F32)
    sc8 = sb("sc8", [128, 8], BF16)
    sr8 = sb("sr8", [8, 128], BF16)
    o32 = sb("o32", [NH, Q], F32)
    selcat = sb("selcat", [66, NH * Q], BF16)
    ldtcat = sb("ldtcat", [66, 2 * Q], BF16)

    arena = Carver(p, 52160)

    pD0 = ps("pD0", [128, 1024], F32)
    pD1 = ps("pD1", [128, 1024], F32)
    pY = ps("pY", [128, 1024], F32)
    pT = ps("pT", [128, 2048], BF16)
    pD0L, pD0R = rg(pD0, "pD0L"), rg(pD0, "pD0R")
    pD1L, pD1R = rg(pD1, "pD1L"), rg(pD1, "pD1R")
    pYL, pYR = rg(pY, "pYL"), rg(pY, "pYR")
    pT0, pT1 = rg(pT, "pT0"), rg(pT, "pT1")
    slot_t = [pD0L, pD0R, pD1L, pD1R, pYL, pYR]
    slot_a = [pD0.h[:, 0:512], pD0.h[:, 512:1024], pD1.h[:, 0:512],
              pD1.h[:, 512:1024], pY.h[:, 0:512], pY.h[:, 512:1024]]

    p.dma("y", ident.h[:], ident_h, [], [ident])
    p.warm_ap = ident.h[:, 0:2]
    p.dma("y", mask01.h[:], mask_h, [], [mask01])
    p.dma("y", mneg.h[:], mneg_h, [], [mneg])
    p.dma("y", orow.h[:], orow_h, [], [orow])
    p.dma("y", ocol.h[:], ocol_h, [], [ocol])
    p.dma("y", oc32.h[:], oc32_h, [], [oc32])
    p.dma("y", or32.h[:], or32_h, [], [or32])
    p.dma("y", o32.h[:], o32_h, [], [o32])
    p.dma("y", selcat.h[0:64, :], sel_h, [], [selcat])
    p.memset("v", ldtcat.h[64:66, :], 1.0, [ldtcat])
    for l in range(NL):
        p.dma("y", convbb[l].h[:], convb_h[l], [], [convbb[l]])
        p.dma("y", convw[l].h[:], convw_h[l], [], [convw[l]])
        p.dma("y", lnw[l].h[:], lnw_h[l], [], [lnw[l]])
        p.dma("y", lnb[l].h[:], lnb_h[l], [], [lnb[l]])
        p.dma("y", dtb[l].h[:], dtb_h[l].unsqueeze(1), [], [dtb[l]])
        p.dma("y", nA[l].h[:], alog_h[l].unsqueeze(1), [], [nA[l]])
        p.dma("g", dprow[l].h[:], dp_h[l], [], [dprow[l]])
        p.dma("y", nwv[l].h[:], nw_h[l], [], [nwv[l]])
        p.act("s", nA[l].h[:], nA[l].h[:], AF.Exp, [nA[l]], [nA[l]])
        p.ts("v", nA[l].h[:], nA[l].h[:], -1.0, None, A.mult, None, [nA[l]], [nA[l]])
        p.memset("v", hst[l].h[hro[l]:hro[l] + 64, :], 0.0, [hst[l]])
        p.memset("v", halo[l].h[:], 0.0, [halo[l]])
    p.dma("y", b1c.h[:], b1_h, [], [b1c])
    p.dma("g", b2row.h[:], b2_h, [], [b2row])
    p.mmg([(pD0.h[:, 0:512], orow.h[0:1, :], b2row.h[0:1, 0:512], True, True)],
          [b2row, orow], [pD0L])
    p.mmg([(pD0.h[:, 512:1024], orow.h[0:1, :], b2row.h[0:1, 512:1024], True, True)],
          [b2row, orow], [pD0R])
    p.cp("v", b2bc.h[:], pD0.h[:, 0:1024], [pD0L, pD0R], [b2bc])
    p.dma("y", g1c.h[:], ln1w_h, [], [g1c])
    p.dma("y", e1c.h[:], ln1b_h, [], [e1c])
    p.dma("y", g2c.h[:], ln2w_h, [], [g2c])
    p.dma("y", e2c.h[:], ln2b_h, [], [e2c])

    def phase0(o):
        ob = o * OT
        for tti in range(OT // 128):
            tb = tti * 128
            p.dma("y", stage.h[:], x_h[ob + tb: ob + tb + 128, :], [], [stage])
            p.red("v", mu128.h[:], stage.h[:], "sum", [stage], [mu128])
            p.act("s", stg16.h[:], stage.h[:], AF.Square, [stage], [stg16])
            p.red("v", sd128.h[:], stg16.h[:], "sum", [stg16], [sd128])
            p.ts("v", mu128.h[:], mu128.h[:], 1.0 / D, None, A.mult, None,
                 [mu128], [mu128])
            p.ts("v", sd128.h[:], sd128.h[:], 1.0 / D, None, A.mult, None,
                 [sd128], [sd128])
            p.tt("v", v128.h[:], mu128.h[:], mu128.h[:], A.mult, [mu128], [v128])
            p.tt("v", sd128.h[:], sd128.h[:], v128.h[:], A.subtract,
                 [sd128, v128], [sd128])
            p.act("s", sd128.h[:], sd128.h[:], AF.Ln, [sd128], [sd128], bias=EPS)
            p.act("s", sd128.h[:], sd128.h[:], AF.Exp, [sd128], [sd128], scale=-0.5)
            p.ts("v", stg16.h[:], stage.h[:], mu128.h[:], sd128.h[:],
                 A.subtract, A.mult, [stage, mu128, sd128], [stg16])
            trs = [(pT.h[:, j * 128:(j + 1) * 128], stg16.h[:, j * 128:(j + 1) * 128],
                    ident.h[:]) for j in range(8)]
            p.trg(trs, [stg16, ident], [pT0, pT1])
            p.cp("v", svw(xseg.h, tb, OT, 8, 128),
                 pT.h[:, 0:1024].rearrange("p (n w) -> p n w", n=8),
                 [pT0, pT1], [xseg])


    phase0(0)

    _bcnt = 0
    for l in range(NL):
        for k in range(8):
            for wv, (c0, c1) in enumerate(((0, 2048), (2048, EIN))):
                bo, bt = (0, wbA) if _bcnt % 2 == 0 else (4 * 2208, wbB)
                _bcnt += 1
                p.dma("g", wbig.h[:, bo:bo + c1 - c0],
                      in_w_h[l, k * 128:(k + 1) * 128, c0:c1], [], [bt])
                p.dma("y", inw_s[l, k * 128:(k + 1) * 128, c0:c1],
                      wbig.h[:, bo:bo + c1 - c0], [bt], [NONE])
        for k in range(16):
            bo, bt = (0, wbA) if _bcnt % 2 == 0 else (4 * 2208, wbB)
            _bcnt += 1
            p.dma("g", wbig.h[:, bo:bo + 1024],
                  out_w_h[l, k * 128:(k + 1) * 128, :], [], [bt])
            p.dma("y", outw_s[l, k * 128:(k + 1) * 128, :],
                  wbig.h[:, bo:bo + 1024], [bt], [NONE])

    def quant_weights(w_src, n_rows, n_cols, scratch, wiv_t, wiv128_t):
        nrc = n_rows // 128
        first = True
        for k in range(nrc):
            for cc in range(n_cols // 1024):
                p.dma("y", stage.h[:], w_src[k * 128:(k + 1) * 128,
                                             cc * 1024:(cc + 1) * 1024], [], [stage])
                p.red("v", v128.h[:], stage.h[:], "sum", [stage], [v128], absval=True)
                if first:
                    p.cp("v", wqs.h[:], v128.h[:], [v128], [wqs])
                    first = False
                else:
                    p.tt("v", wqs.h[:], wqs.h[:], v128.h[:], A.add, [wqs, v128], [wqs])
        p.mmg([(pY.h[0:1, 0:1], oc32.h[:], wqs.h[:], True, True)], [wqs, oc32], [pYL])
        p.cp("v", wqr.h[:], pY.h[0:1, 0:1], [pYL], [wqr])
        p.ts("v", wiv_t.h[:], wqr.h[:], 1.0 / (n_rows * n_cols), EPS,
             A.mult, A.max, [wqr], [wiv_t])
        p.mmg([(pY.h[:, 0:1], or32.h[:], wqr.h[:], True, True)], [wqr, or32], [pYL])
        p.ts("v", wiv128_t.h[:], pY.h[:, 0:1], 1.0 / (n_rows * n_cols), EPS,
             A.mult, A.max, [pYL], [wiv128_t])
        p.cp("v", wqr128.h[:], wiv128_t.h[:], [wiv128_t], [wqr128])
        p.recip("v", wqr128.h[:], wqr128.h[:], [wqr128], [wqr128])
        ivw = stage.h[:].bitcast(I32)[:, 0:1024]
        for k in range(nrc):
            for cc in range(n_cols // 1024):
                p.dma("y", stage.h[:], w_src[k * 128:(k + 1) * 128,
                                             cc * 1024:(cc + 1) * 1024], [], [stage])
                p.ts("v", stage.h[:], stage.h[:], wqr128.h[:], -1.49, A.mult, A.max,
                     [stage, wqr128], [stage])
                p.ts("v", stage.h[:], stage.h[:], 1.49, MAGIC, A.min, A.add,
                     [stage], [stage])
                p.ts("v", ivw, stage.h[:], MAGIC - 1024.0, 0.25, A.subtract, A.add,
                     [stage], [stage])
                p.ts("v", stg16.h[:, 0:1024], ivw, 1024.0, None, A.subtract, None,
                     [stage], [stg16])
                trs = [(pT.h[:, j * 128:(j + 1) * 128],
                        stg16.h[:, j * 128:(j + 1) * 128], ident.h[:])
                       for j in range(8)]
                p.trg(trs, [stg16, ident], [pT0, pT1])
                p.cp("v", bcast.h[:, 0:1024], pT.h[:, 0:1024], [pT0, pT1], [bcast])
                p.dma("y",
                      scratch[cc * 1024:(cc + 1) * 1024, k * 128:(k + 1) * 128]
                      .rearrange("(a p) b -> p a b", p=128),
                      bcast.h[:, 0:1024].rearrange("p (a b) -> p a b", a=8),
                      [bcast], [NONE])

    _quant_todo = [(w1_h, HID, D, qw1_s, wiv[0], wiv128[0]),
                   (w2_h, D, HID, qw2_s, wiv[1], wiv128[1])]

    store_toks = []

    def fm_stats(src_t, apfn, nf):
        mms = [(pY.h[0:1, 0:TT], ocol.h[:], apfn(j), j == 0, j == nf - 1)
               for j in range(nf)]
        p.mmg(mms, [src_t, ocol], [pYL])
        bufs = (sq, sq2)
        for j in range(nf):
            b = bufs[j % 2]
            p.act("s", b.h[:, 0:TT], apfn(j), AF.Square, [src_t], [b])
            p.mmg([(pY.h[32:33, 512:512 + TT], ocol.h[:], b.h[:, 0:TT],
                    j == 0, j == nf - 1)], [b, ocol], [pYR])

    def rstd_mu(nfe):
        p.ts("v", mrow.h[:], pY.h[0:1, 0:TT], 1.0 / nfe, None, A.mult, None,
             [pYL], [mrow])
        p.ts("v", srow.h[:], pY.h[32:33, 512:512 + TT], 1.0 / nfe, None, A.mult,
             None, [pYR], [srow])
        p.tt("v", rrow.h[:], mrow.h[:], mrow.h[:], A.mult, [mrow], [rrow])
        p.tt("v", srow.h[:], srow.h[:], rrow.h[:], A.subtract, [srow, rrow], [srow])
        p.act("s", rrow.h[:], srow.h[:], AF.Ln, [srow], [rrow], bias=EPS)
        p.act("s", rrow.h[:], rrow.h[:], AF.Exp, [rrow], [rrow], scale=-0.5)

    def bcast_rows(r0_t, r0_ap, r1_t, r1_ap):
        p.cp("v", brow.h[0:1, 0:TT], r0_ap, [r0_t], [brow])
        p.cp("v", brow.h[0:1, TT:2 * TT], r1_ap, [r1_t], [brow])
        p.mmg([(pD1.h[:, 0:512], orow.h[0:1, :], brow.h[0:1, 0:512], True, True)],
              [brow, orow], [pD1L])
        p.mmg([(pD1.h[:, 512:1024], orow.h[0:1, :], brow.h[0:1, 512:1024], True, True)],
              [brow, orow], [pD1R])
        p.cp("v", bcast.h[:, 0:1024], pD1.h[:, 0:1024], [pD1L, pD1R], [bcast])

    def apply_ln(src_t, src_view, out_t, out_view, nf, gam, bet):
        p.tt("v", out_view, src_view, bcv(bcast.h[:, TT:2 * TT], nf), A.subtract,
             [src_t, bcast], [out_t])
        p.tt("v", out_view, out_view, bcv(bcast.h[:, 0:TT], nf), A.mult,
             [out_t, bcast], [out_t])
        if gam is not None:
            for j in range(nf):
                osl = out_view[:, j, :] if False else None
            return

    _sqm_holder = {}

    def qchain(xt, nf, wiv_t, wiv128_t):
        sqm = _sqm_holder["t"]
        """Act-quant per token over nf*128 features (feature-major), in place.
        bcast[:,0:TT] <- 127/max (quant scale), bcast[:,TT:2TT] <- max/127*wiv."""
        sqm2 = [rg(sqm, "sqmP"), rg(sqm, "sqmN")]
        for j in range(nf):
            a = xt.h[:, j * TT:(j + 1) * TT]
            for sl_i, base_op in ((0, A.max), (1, A.min)):
                r = sqm.h[:, sl_i * TT:(sl_i + 1) * TT]
                if j == 0:
                    p.cp("v", r, a, [xt], [sqm2[sl_i]])
                else:
                    p.tt("v", r, r, a, base_op, [xt, sqm2[sl_i]], [sqm2[sl_i]])
        p.ts("v", sqm.h[:, TT:2 * TT], sqm.h[:, TT:2 * TT], -1.0, None,
             A.mult, None, [sqm2[1]], [sqm2[1]])
        p.tt("v", sq.h[:], sqm.h[:, 0:TT], sqm.h[:, TT:2 * TT], A.max,
             [sqm2[0], sqm2[1]], [sq])
        trs = [(pT.h[:, 128 * tq:128 * (tq + 1)], sq.h[:, tq * 128:(tq + 1) * 128],
                ident.h[:]) for tq in range(4)]
        p.trg(trs, [sq, ident], [pT0])
        for tq in range(4):
            p.red("v", mq4.h[:, tq:tq + 1], pT.h[:, 128 * tq:128 * (tq + 1)],
                  "max", [pT0], [mq4])
        p.ts("v", mq4.h[:], mq4.h[:], EPS, None, A.max, None, [mq4], [mq4])
        p.act("s", mt4.h[:], mq4.h[:], AF.Ln, [mq4], [mt4])
        p.act("s", mt4.h[:], mt4.h[:], AF.Exp, [mt4], [mt4], scale=-1.0,
              bias=4.844187086)
        p.cp("v", sc8.h[:, 0:4], mt4.h[:], [mt4], [sc8])
        p.ts("v", mq4.h[:], mq4.h[:], wiv128_t.h[:], 1.0 / 127.0, A.mult, A.mult,
             [mq4, wiv128_t], [mq4])
        p.cp("v", sc8.h[:, 4:8], mq4.h[:], [mq4], [sc8])
        p.trg([(pT.h[0:8, 64:64 + 128], sc8.h[:], ident.h[:])], [sc8, ident], [pT0])
        p.cp("v", sr8.h[:], pT.h[0:8, 64:64 + 128], [pT0], [sr8])
        p.dma("y", brow.h[0:1, :], sr8.h[:], [sr8], [brow])
        p.mmg([(pD1.h[:, 0:512], orow.h[0:1, :], brow.h[0:1, 0:512], True, True)],
              [brow, orow], [pD1L])
        p.mmg([(pD1.h[:, 512:1024], orow.h[0:1, :], brow.h[0:1, 512:1024],
                True, True)], [brow, orow], [pD1R])
        p.cp("v", bcast.h[:, 0:1024], pD1.h[:, 0:1024], [pD1L, pD1R], [bcast])
        for s0 in range(0, nf, 4):
            ns = min(4, nf - s0)
            xv = svw(xt.h, s0 * TT, TT, ns, TT)
            fv = fq.h[:, 0:ns * TT].rearrange("p (n w) -> p n w", n=ns)
            p.tt("v", fv, xv, bcv(bcast.h[:, 0:TT], ns), A.mult, [xt, bcast], [fq])
            fa = fq.h[:, 0:ns * TT]
            p.ts("v", fa, fa, -128.49, 127.49, A.max, A.min, [fq], [fq])
            p.ts("v", fa, fa, MAGIC, MAGIC - 1024.0, A.add, A.subtract, [fq], [fq])
            p.ts("v", iqt.h[:, 0:ns * TT], fa, 0.25, None, A.add, None, [fq], [iqt])
            p.ts("v", xt.h[:, s0 * TT:(s0 + ns) * TT], iqt.h[:, 0:ns * TT], 1024.0,
                 None, A.subtract, None, [iqt], [xt])

    for o in range(NO):
        ob = o * OT
        arena.newgen()
        xn = arena.get("xn", [128, 8 * TT], BF16)
        xnH = []
        for h_ in range(2):
            t = Tk(xn.h[:, h_ * 4 * TT:(h_ * 4 + 4) * TT], f"xn{h_}")
            t.r = dict(xn.r)
            xnH.append(t)
        zt = arena.get("zt", [128, ZC * TT + TT], BF16)
        convin = arena.get("convin", [128, XC * CIW + 8], BF16)
        cvinJ = []
        for j in range(XC):
            ce = (j + 1) * CIW + (8 if j == XC - 1 else 0)
            t = Tk(convin.h[:, j * CIW:ce], f"cvin{j}")
            t.r = dict(convin.r)
            cvinJ.append(t)
        convout = arena.get("convout", [128, XC * TT], BF16)
        cvJ = []
        for j in range(XC):
            t = Tk(convout.h[:, j * TT:(j + 1) * TT], f"cv{j}")
            t.r = dict(convout.r)
            cvJ.append(t)
        dpi = arena.get("dpi", [128, NH * Q], BF16)
        stok = arena.get("stok", [128, NH * Q // 2], BF16)
        ssta = arena.get("ssta", [64, NH * Q // 2], BF16)
        xsT = arena.get("xsT", [128, DIN], BF16)
        xbar2 = [arena.get(f"xbar{i}", [128, DIN], BF16) for i in range(2)]
        dtraw = arena.get("dtraw", [NH, TT], F32)
        dtv = arena.get("dtv", [NH, TT], F32)
        ldt = dtv
        dtA = dtraw
        cbuf2 = [arena.get(f"cbuf{i}", [NH, Q], F32) for i in range(2)]
        rbuf2 = [arena.get(f"rbuf{i}", [NH, Q], F32) for i in range(2)]
        chi2 = [arena.get(f"chi{i}", [NH, Q], BF16) for i in range(2)]
        clo2 = [arena.get(f"clo{i}", [NH, Q], BF16) for i in range(2)]
        fb162 = [arena.get(f"fb16{i}", [NH, Q], BF16) for i in range(2)]
        fTs2 = [arena.get(f"fTs{i}", [128, NH], BF16) for i in range(2)]
        lrow2 = [arena.get(f"lrow{i}", [1, NH], BF16) for i in range(2)]
        scT2 = [arena.get(f"scT{i}", [128, Q], BF16) for i in range(2)]
        bTs2 = [arena.get(f"bTs{i}", [128, DST], BF16) for i in range(2)]
        cN2 = [arena.get(f"cN{i}", [64, Q], BF16) for i in range(2)]

        if o > 0:
            phase0(o)

        for l in range(NL):
            if o == 0 and l in (1, 2) and _quant_todo:
                quant_weights(*_quant_todo.pop(0))
            p.mmg([(pY.h[:, 0:NH], orow.h[0:1, :], dprow[l].h[:], True, True)],
                  [orow, dprow[l]], [pYL])
            for h in range(NH):
                p.ts("v", dpi.h[:, h * Q:(h + 1) * Q], ident.h[:],
                     pY.h[:, h:h + 1], None, A.mult, None, [ident, pYL], [dpi])
            for it in range(NI):
                itb = it * TT
                xap = lambda j: xseg.h[:, j * OT + itb: j * OT + itb + TT]
                fm_stats(xseg, xap, 8)
                rstd_mu(1024.0)
                bcast_rows(rrow, rrow.h[:], mrow, mrow.h[:])
                for h_ in range(2):
                    xv = svw(xseg.h, h_ * 4 * OT + itb, OT, 4, TT)
                    ov = (xn.h[:, h_ * 4 * TT:(h_ * 4 + 4) * TT]
                          .rearrange("p (n w) -> p n w", n=4))
                    p.tt("v", ov, xv, bcv(bcast.h[:, TT:2 * TT], 4), A.subtract,
                         [xseg, bcast], [xnH[h_]])
                    p.tt("v", ov, ov, bcv(bcast.h[:, 0:TT], 4), A.mult,
                         [xnH[h_], bcast], [xnH[h_]])
                    for j in range(h_ * 4, h_ * 4 + 4):
                        osl = xn.h[:, j * TT:(j + 1) * TT]
                        p.ts("v", osl, osl, lnw[l].h[:, j:j + 1], lnb[l].h[:, j:j + 1],
                             A.mult, A.add, [xnH[h_], lnw[l], lnb[l]], [xnH[h_]])
                p.cp("v",
                     convin.h[:, 0:XC * CIW].rearrange("p (j w) -> p j w", j=XC)[:, :, 0:3],
                     halo[l].h[:].rearrange("p (j w) -> p j w", j=XC),
                     [halo[l]], cvinJ)
                for k in range(8):
                    p.dma("g", wbig.h[:, k * 2208:(k + 1) * 2208],
                          inw_s[l, k * 128:(k + 1) * 128, 2048:EIN], [],
                          [wbA if k < 4 else wbB])
                for eg in (range(16, 22), range(22, 28), range(28, EC)):
                    for e in eg:
                        ew = 128 if e < 33 else 32
                        sa = slot_a[e % 6]
                        eo = (e - 16) * 128
                        mms = [(sa[0:ew, :] if ew < 128 else sa,
                                wbig.h[:, k * 2208 + eo: k * 2208 + eo + ew],
                                xn.h[:, k * TT:(k + 1) * TT], k == 0, False)
                               for k in range(4)]
                        p.mmg(mms, [wbA, xnH[0]], [slot_t[e % 6]])
                    for e in eg:
                        ew = 128 if e < 33 else 32
                        sl, sa = slot_t[e % 6], slot_a[e % 6]
                        eo = (e - 16) * 128
                        mms = [(sa[0:ew, :] if ew < 128 else sa,
                                wbig.h[:, k * 2208 + eo: k * 2208 + eo + ew],
                                xn.h[:, k * TT:(k + 1) * TT], False, k == 7)
                               for k in range(4, 8)]
                        p.mmg(mms, [wbB, xnH[1]], [sl])
                        if e < 33:
                            j = e - 16
                            p.cp("s", convin.h[:, j * CIW + 3: j * CIW + 3 + TT], sa,
                                 [sl], [cvinJ[j]])
                        else:
                            p.cp("s", dtraw.h, sa[0:NH, :], [sl], [dtraw])
                for k in range(8):
                    p.dma("g", wbig.h[:, k * 2208:k * 2208 + 2048],
                          inw_s[l, k * 128:(k + 1) * 128, 0:2048], [],
                          [wbA if k < 4 else wbB])
                for eg in (range(0, 6), range(6, 12), range(12, 16)):
                    for e in eg:
                        sa = slot_a[e % 6]
                        mms = [(sa, wbig.h[:, k * 2208 + e * 128:
                                           k * 2208 + (e + 1) * 128],
                                xn.h[:, k * TT:(k + 1) * TT], k == 0, False)
                               for k in range(4)]
                        p.mmg(mms, [wbA, xnH[0]], [slot_t[e % 6]])
                    for e in eg:
                        sl, sa = slot_t[e % 6], slot_a[e % 6]
                        mms = [(sa, wbig.h[:, k * 2208 + e * 128:
                                           k * 2208 + (e + 1) * 128],
                                xn.h[:, k * TT:(k + 1) * TT], False, k == 7)
                               for k in range(4, 8)]
                        p.mmg(mms, [wbB, xnH[1]], [sl])
                        p.act("s", zt.h[:, e * TT:(e + 1) * TT], sa, AF.Silu, [sl], [zt])
                for k in range(4):
                    wdep = [[wbA], [wbA], [wbA, wbB], [wbB]][k]
                    p.dma("g", wbig.h[:, k * 4096:(k + 1) * 4096]
                          .rearrange("p (a b) -> p a b", a=4),
                          outw_s[l, k * 512:(k + 1) * 512, :]
                          .rearrange("(a p) b -> p a b", p=128), [], wdep)
                for j in range(XC):
                    cb = j * CIW
                    p.ts("v", cvJ[j].h,
                         convin.h[:, cb: cb + TT], convw[l].h[:, j * 4:j * 4 + 1],
                         convbb[l].h[:, j:j + 1], A.mult, A.add,
                         [cvinJ[j], convw[l], convbb[l]], [cvJ[j]])
                for k in range(1, 4):
                    for j in range(XC):
                        cb = j * CIW
                        p.stt("v", cvJ[j].h,
                              convin.h[:, cb + k: cb + k + TT],
                              convw[l].h[:, j * 4 + k:j * 4 + k + 1],
                              cvJ[j].h, A.mult, A.add,
                              [cvinJ[j], convw[l], cvJ[j]], [cvJ[j]])
                p.cp("v", halo[l].h[:].rearrange("p (j w) -> p j w", j=XC),
                     convin.h[:, 0:XC * CIW].rearrange("p (j w) -> p j w", j=XC)[:, :, TT:TT + 3],
                     cvinJ, [halo[l]])
                cv3 = convout.h.rearrange("p (n s) -> p n s", n=XC)
                for qq in range(TT // Q):
                    cq = cv3[:, :, qq * Q:(qq + 1) * Q]
                    p.act("s", cq, cq, AF.Silu, cvJ, cvJ)
                p.act("s", dtv.h, dtraw.h, AF.Exp, [dtraw, dtb[l]], [dtv],
                      bias=dtb[l].h[:])
                p.act("s", dtv.h, dtv.h, AF.Ln, [dtv], [dtv], bias=1.0)
                p.ts("v", dtA.h, dtv.h, nA[l].h[:], None, A.mult, None,
                     [dtv, nA[l]], [dtA])
                p.act("s", ldt.h, dtv.h, AF.Ln, [dtv, dtA], [ldt])
                ldtc2 = [rg(ldtcat, f"ldtc{i}") for i in range(2)]
                selc2 = [rg(selcat, f"selc{i}") for i in range(2)]

                def cpath(q, pr):
                    qb = q * Q
                    cbuf, rbuf = cbuf2[pr], rbuf2[pr]
                    chi, clo, fb16 = chi2[pr], clo2[pr], fb162[pr]
                    fTs, lrow, scT, bTs, cN = (fTs2[pr], lrow2[pr], scT2[pr],
                                               bTs2[pr], cN2[pr])
                    lc = ldtc2[pr]
                    p.scan("v", cbuf.h, o32.h[:], dtA.h[:, qb:qb + Q], 0.0,
                           A.mult, A.add, [o32, dtA], [cbuf])
                    p.tt("v", rbuf.h, ldt.h[:, qb:qb + Q], cbuf.h, A.subtract,
                         [ldt, cbuf], [rbuf])
                    lcap = ldtcat.h[:, pr * Q:(pr + 1) * Q]
                    p.cp("v", lcap[0:NH, :], rbuf.h, [rbuf], [lc])
                    p.tt("v", lcap[NH:2 * NH, :], rbuf.h, lcap[0:NH, :],
                         A.subtract, [rbuf, lc], [lc])
                    p.cp("v", chi.h, cbuf.h, [cbuf], [chi])
                    p.tt("v", clo.h, cbuf.h, chi.h, A.subtract, [cbuf, chi], [clo])
                    bsl = convout.h[0:64, 16 * TT + qb: 16 * TT + qb + Q]
                    csl = convout.h[64:128, 16 * TT + qb: 16 * TT + qb + Q]
                    p.cp("v", cN.h, csl, [cvJ[16]], [cN])
                    p.mmg([(pD1.h[:, 0:Q], bsl, cN.h, True, True)],
                          [cvJ[16], cN], [pD1L])
                    p.tt("v", pD1.h[:, 0:Q], pD1.h[:, 0:Q], mask01.h[:], A.mult,
                         [pD1L, mask01], [pD1L])
                    p.cp("v", scT.h, pD1.h[:, 0:Q], [pD1L], [scT])
                    p.act("s", fb16.h, rbuf.h, AF.Exp, [rbuf, cbuf], [fb16],
                          bias=cbuf.h[:, Q - 1:Q])
                    p.trg([(pT.h[0:Q, 0:NH], fb16.h, ident.h[0:NH, 0:NH])],
                          [fb16, ident], [pT0])
                    p.cp("v", fTs.h[:, 0:NH], pT.h[0:Q, 0:NH], [pT0], [fTs])
                    p.act("s", sq.h[0:NH, 0:1], cbuf.h[:, Q - 1:Q], AF.Exp,
                          [cbuf], [sq])
                    p.trg([(pT.h[0:1, 512:512 + NH], sq.h[0:NH, 0:1],
                            ident.h[0:NH, 0:NH])], [sq, ident], [pT0])
                    p.cp("v", lrow.h, pT.h[0:1, 512:512 + NH], [pT0], [lrow])
                    p.trg([(pT.h[:, 1024:1024 + 64], bsl, ident.h[0:64, 0:64])],
                          [cvJ[16], ident], [pT1])
                    p.cp("v", bTs.h[:, 0:DST], pT.h[:, 1024:1024 + 64], [pT1], [bTs])

                def body(q, pr):
                    qb = q * Q
                    cbuf, rbuf = cbuf2[pr], rbuf2[pr]
                    xbar = xbar2[pr]
                    fTs, lrow, scT, bTs, cN = (fTs2[pr], lrow2[pr], scT2[pr],
                                               bTs2[pr], cN2[pr])
                    p.dma("y", selcat.h[64:65, :], chi2[pr].h, [chi2[pr]], [selcat])
                    p.dma("y", selcat.h[65:66, :], clo2[pr].h, [clo2[pr]], [selcat])
                    lc = ldtc2[pr]
                    lcap = ldtcat.h[:, pr * Q:(pr + 1) * Q]
                    dsl = [pD0L, pD0R, pD1L, pD1R]
                    dsa = [pD0.h[:, 0:512], pD0.h[:, 512:1024],
                           pD1.h[:, 0:512], pD1.h[:, 512:1024]]
                    trs = [(pT.h[:, fc * 128:(fc + 1) * 128],
                            convout.h[:, fc * TT + qb: fc * TT + qb + Q], ident.h[:])
                           for fc in range(ZC)]
                    p.trg(trs, [cvJ[fc] for fc in range(ZC)] + [ident], [pT0, pT1])
                    p.cp("v", xsT.h, pT.h[:, 0:2048], [pT0, pT1], [xsT])
                    p.tt("v", xbar.h[:].rearrange("p (n w) -> p n w", n=NH),
                         xsT.h[:].rearrange("p (n w) -> p n w", n=NH),
                         fTs.h[:, 0:NH].unsqueeze(2).broadcast_to([128, NH, 64]),
                         A.mult, [xsT, fTs], [xbar])
                    dsl_ = [pD0L, pD0R, pD1L, pD1R]
                    for hf in range(2):
                        ho = hf * 16
                        for sub in range(4):
                            pc = hf * 4 + sub
                            p.mmg([(dsa[sub], lcap,
                                    selcat.h[:, pc * 512:(pc + 1) * 512], True, False),
                                   (dsa[sub], ident.h[:], mneg.h[:], False, True)],
                                  [lc, selcat, ident, mneg], [dsl[sub]])
                        p.act("s", stok.h[:, 0:1024], pD0.h[:, 0:1024], AF.Exp,
                              [dsl[0], dsl[1]], [stok])
                        p.act("s", stok.h[:, 1024:2048], pD1.h[:, 0:1024], AF.Exp,
                              [dsl[2], dsl[3]], [stok])
                        if hf == 0 and q > 0:
                            state_upd((q - 1) % 2)
                        for sub in range(4):
                            pc = hf * 4 + sub
                            p.mmg([(dsa[sub][64:128, :], ldtcat.h[64:66, 0:64],
                                    selcat.h[64:66, pc * 512:(pc + 1) * 512],
                                    True, True)],
                                  [ldtcat, selcat], [dsl[sub]])
                        p.act("s", ssta.h[0:64, 0:1024], pD0.h[64:128, 0:1024],
                              AF.Exp, [dsl[0], dsl[1]], [ssta])
                        p.act("s", ssta.h[0:64, 1024:2048], pD1.h[64:128, 0:1024],
                              AF.Exp, [dsl[2], dsl[3]], [ssta])
                        p.stt("v", stok.h[:].rearrange("p (n w) -> p n w", n=16),
                              stok.h[:].rearrange("p (n w) -> p n w", n=16), CAP,
                              bcv(scT.h, 16), A.min, A.mult,
                              [stok, scT], [stok])
                        p.tt("v", stok.h[:], stok.h[:],
                             dpi.h[:, ho * Q:(ho + 16) * Q], A.add,
                             [stok, dpi], [stok])
                        p.tt("v", ssta.h[:].rearrange("p (n w) -> p n w", n=16),
                             ssta.h[:].rearrange("p (n w) -> p n w", n=16),
                             bcv(cN.h, 16), A.mult, [ssta, cN], [ssta])
                        mms = []
                        for pr_ in range(8):
                            for par in range(2):
                                hh = pr_ * 2 + par
                                h = ho + hh
                                tp = (0, 64) if par else None
                                obp = par * 64
                                oap = pY.h[obp:obp + 64, pr_ * Q:(pr_ + 1) * Q]
                                mms.append((oap, xsT.h[:, h * 64:(h + 1) * 64],
                                            stok.h[:, hh * Q:(hh + 1) * Q],
                                            True, False, tp))
                                mms.append((oap, hst[l].h[hro[l]:hro[l] + 64,
                                                          h * 64:(h + 1) * 64],
                                            ssta.h[:, hh * Q:(hh + 1) * Q], False,
                                            True, tp))
                        p.mmg(mms, [xsT, stok, hst[l], ssta],
                              [pYL, pYR])
                        zv = svw(zt.h, (hf * 8) * TT + qb, TT, 8, Q)
                        yv = svw(convin.h, (hf * 8) * CIW + 3 + qb, CIW, 8, Q)
                        p.tt("v", yv,
                             pY.h[:, 0:1024].rearrange("p (n w) -> p n w", n=8),
                             zv, A.mult, [pYL, pYR, zt],
                             [cvinJ[hf * 8 + i] for i in range(8)])
                        if hf == 0 and q + 1 < NQ:
                            cpath(q + 1, (q + 1) % 2)
                def state_upd(pr):
                    lrowS, bTsS, xbarS = lrow2[pr], bTs2[pr], xbar2[pr]
                    lam_v = [lrowS.h[0:1, g * 8:(g + 1) * 8]
                             .unsqueeze(2).broadcast_to([1, 8, 64]) for g in range(4)]
                    for half in range(2):
                        hb = half * 1024
                        p.mmg([(pY.h[0:64, 0:512], orow.h[0:1, 0:64],
                                lam_v[half * 2], True, True)], [orow, lrowS], [pYL])
                        p.mmg([(pY.h[0:64, 512:1024], orow.h[0:1, 0:64],
                                lam_v[half * 2 + 1], True, True)], [orow, lrowS], [pYR])
                        p.tt("v", hst[l].h[hro[l]:hro[l] + 64, hb:hb + 1024],
                             hst[l].h[hro[l]:hro[l] + 64, hb:hb + 1024],
                             pY.h[0:64, 0:1024], A.mult, [hst[l], pYL, pYR],
                             [hst[l]])
                        p.mmg([(pY.h[0:64, 0:512], bTsS.h[:, 0:DST],
                                xbarS.h[:, hb:hb + 512], True, True)],
                              [bTsS, xbarS], [pYL])
                        p.mmg([(pY.h[0:64, 512:1024], bTsS.h[:, 0:DST],
                                xbarS.h[:, hb + 512:hb + 1024], True, True)],
                              [bTsS, xbarS], [pYR])
                        p.tt("v", hst[l].h[hro[l]:hro[l] + 64, hb:hb + 1024],
                             hst[l].h[hro[l]:hro[l] + 64, hb:hb + 1024],
                             pY.h[0:64, 0:1024], A.add, [hst[l], pYL, pYR],
                             [hst[l]])

                cpath(0, 0)
                for q in range(NQ):
                    body(q, q % 2)
                state_upd((NQ - 1) % 2)
                yap = lambda j: convin.h[:, j * CIW + 3: j * CIW + 3 + TT]
                for j in range(ZC):
                    b = (sq, sq2)[j % 2]
                    p.act("s", b.h[:, 0:TT], yap(j), AF.Square, [cvinJ[j]], [b])
                    p.mmg([(pY.h[0:1, 0:TT], ocol.h[:], b.h[:, 0:TT],
                            j == 0, j == ZC - 1)], [b, ocol], [pYL])
                p.ts("v", srow.h[:], pY.h[0:1, 0:TT], 1.0 / DIN, None, A.mult, None,
                     [pYL], [srow])
                p.act("s", rrow.h[:], srow.h[:], AF.Ln, [srow], [rrow], bias=EPS)
                p.act("s", rrow.h[:], rrow.h[:], AF.Exp, [rrow], [rrow], scale=-0.5)
                p.cp("v", brow.h[0:1, 0:TT], rrow.h[:], [rrow], [brow])
                p.mmg([(pD1.h[:, 0:512], orow.h[0:1, :], brow.h[0:1, 0:TT],
                        True, True)], [brow, orow], [pD1L])
                p.cp("v", bcast.h[:, 0:TT], pD1.h[:, 0:512], [pD1L], [bcast])
                yv_all = svw(convin.h, 3, CIW, ZC, TT)
                p.tt("v", yv_all, yv_all, bcv(bcast.h[:, 0:TT], ZC), A.mult,
                     cvinJ[:ZC] + [bcast], cvinJ[:ZC])
                for j in range(ZC):
                    p.ts("v", yap(j), yap(j), nwv[l].h[:, j:j + 1], None, A.mult,
                         None, [cvinJ[j], nwv[l]], [cvinJ[j]])
                for dg in range(2):
                    for k in range(16):
                        mms = [(slot_a[di], wbig.h[:, k * 1024 + (dg * 4 + di) * 128:
                                                   k * 1024 + (dg * 4 + di + 1) * 128],
                                yap(k), k == 0, k == 15) for di in range(4)]
                        rdep = [wbA] if k < 8 else ([wbA, wbB] if k == 8 else [wbB])
                        p.mmg(mms, rdep + [cvinJ[k]],
                              [slot_t[0], slot_t[1], slot_t[2], slot_t[3]])
                    for di in range(4):
                        d = dg * 4 + di
                        xa = xseg.h[:, d * OT + itb: d * OT + itb + TT]
                        p.tt("v", xa, slot_a[di], xa, A.add,
                             [slot_t[di], xseg], [xseg])

        for tti in range(OT // 128):
            tb = tti * 128
            p.dma("y", stage.h[:], x_h[ob + tb: ob + tb + 128, :], [], [stage])
            p.cp("v", stg16.h[:], stage.h[:], [stage], [stg16])
            trs = [(pT.h[:, j * 128:(j + 1) * 128], stg16.h[:, j * 128:(j + 1) * 128],
                    ident.h[:]) for j in range(8)]
            p.trg(trs, [stg16, ident], [pT0, pT1])
            xa = svw(xseg.h, tb, OT, 8, 128)
            p.tt("v", xa, xa, pT.h[:, 0:1024].rearrange("p (n w) -> p n w", n=8),
                 A.add, [xseg, pT0, pT1], [xseg])

        arena.newgen()
        fA = arena.get("fA", [128, 8 * TT], BF16)
        fB = arena.get("fB", [128, 8 * TT], BF16)
        gl = arena.get("gl", [128, 32 * TT], BF16)
        fC = arena.get("fC", [128, 32 * TT], BF16)
        iqt = arena.get("iqt", [128, 4 * TT], I32)
        fq = arena.get("fq", [128, 4 * TT], F32)
        sqm = arena.get("sqm", [128, 2 * TT], BF16)
        _sqm_holder["t"] = sqm

        for it in range(NI):
            itb = it * TT
            xap = lambda j: xseg.h[:, j * OT + itb: j * OT + itb + TT]
            fap = lambda j: fA.h[:, j * TT:(j + 1) * TT]
            bap = lambda j: fB.h[:, j * TT:(j + 1) * TT]
            gap = lambda j: gl.h[:, j * TT:(j + 1) * TT]
            cap = lambda j: fC.h[:, j * TT:(j + 1) * TT]
            fm_stats(xseg, xap, 8)
            rstd_mu(1024.0)
            bcast_rows(rrow, rrow.h[:], mrow, mrow.h[:])
            xv = svw(xseg.h, itb, OT, 8, TT)
            av = fA.h[:].rearrange("p (n w) -> p n w", n=8)
            p.tt("v", av, xv, bcv(bcast.h[:, TT:2 * TT], 8), A.subtract,
                 [xseg, bcast], [fA])
            p.tt("v", av, av, bcv(bcast.h[:, 0:TT], 8), A.mult, [fA, bcast], [fA])
            fm_stats(fA, fap, 8)
            rstd_mu(1024.0)
            bcast_rows(rrow, rrow.h[:], mrow, mrow.h[:])
            bv = fB.h[:].rearrange("p (n w) -> p n w", n=8)
            p.tt("v", bv, av, bcv(bcast.h[:, TT:2 * TT], 8), A.subtract,
                 [fA, bcast], [fB])
            p.tt("v", bv, bv, bcv(bcast.h[:, 0:TT], 8), A.mult, [fB, bcast], [fB])
            for j in range(8):
                p.ts("v", bap(j), bap(j), g1c.h[:, j:j + 1], e1c.h[:, j:j + 1],
                     A.mult, A.add, [fB, g1c, e1c], [fB])
            qchain(fB, 8, wiv[0], wiv128[0])
            for hw in range(2):
                hco = hw * 16
                for k in range(8):
                    p.dma("g", wbig.h[:, k * 2048:(k + 1) * 2048],
                          qw1_s[k * 128:(k + 1) * 128, hw * 2048:(hw + 1) * 2048],
                          [], [wbA, wbB])
                for hc0 in range(16):
                    hc = hco + hc0
                    sl, sa = slot_t[hc % 6], slot_a[hc % 6]
                    mms = [(sa, wbig.h[:, k * 2048 + hc0 * 128: k * 2048 + (hc0 + 1) * 128],
                            bap(k), k == 0, k == 7) for k in range(8)]
                    p.mmg(mms, [wbA, wbB, fB], [sl])
                    p.tt("v", stage.h[:, 0:TT], sa, bcast.h[:, TT:2 * TT], A.mult,
                         [sl, bcast], [stage])
                    p.act("s", gap(hc), stage.h[:, 0:TT], AF.Gelu, [stage, b1c], [gl],
                          bias=b1c.h[:, hc:hc + 1])
            fm_stats(gl, gap, 32)
            rstd_mu(4096.0)
            bcast_rows(rrow, rrow.h[:], mrow, mrow.h[:])
            gv = gl.h[:].rearrange("p (n w) -> p n w", n=32)
            cv = fC.h[:].rearrange("p (n w) -> p n w", n=32)
            p.tt("v", cv, gv, bcv(bcast.h[:, TT:2 * TT], 32), A.subtract,
                 [gl, bcast], [fC])
            p.tt("v", cv, cv, bcv(bcast.h[:, 0:TT], 32), A.mult, [fC, bcast], [fC])
            for j in range(32):
                p.ts("v", cap(j), cap(j), g2c.h[:, j:j + 1], e2c.h[:, j:j + 1],
                     A.mult, A.add, [fC, g2c, e2c], [fC])
            qchain(fC, 32, wiv[1], wiv128[1])
            p.cp("v", invc.h[:], mq4.h[:], [mq4], [invc])
            for dh in range(2):
                for k in range(8):
                    p.dma("g",
                          wbig.h[:, k * 2048:(k + 1) * 2048].rearrange("p (a b) -> p a b", a=4),
                          qw2_s[k * 512:(k + 1) * 512, dh * 512:(dh + 1) * 512]
                          .rearrange("(a p) b -> p a b", p=128), [], [wbA, wbB])
                for tq in range(4):
                    tqb = itb + tq * 128
                    trs = [(pT.h[:, 1024 + j * 128: 1024 + (j + 1) * 128],
                            xseg.h[:, (dh * 4 + j) * OT + tqb: (dh * 4 + j) * OT + tqb + 128],
                            ident.h[:]) for j in range(4)]
                    p.trg(trs, [xseg, ident], [pT1])
                    sl, sa = slot_t[2 + (tq % 2)], slot_a[2 + (tq % 2)]
                    mms = []
                    for k in range(32):
                        lhs = fC.h[:, k * TT + tq * 128: k * TT + tq * 128 + 128]
                        rhs = wbig.h[:, (k // 4) * 2048 + (k % 4) * 512:
                                     (k // 4) * 2048 + (k % 4) * 512 + 512]
                        mms.append((sa, lhs, rhs, k == 0, k == 31))
                    p.mmg(mms, [fC, wbA, wbB], [sl])
                    so = stage.h[:, (tq % 2) * 512:((tq % 2) + 1) * 512]
                    p.ts("v", so, sa, invc.h[:, tq:tq + 1], None, A.mult, None,
                         [sl, invc], [stage])
                    p.tt("v", so, so, b2bc.h[:, dh * 512:(dh + 1) * 512], A.add,
                         [stage, b2bc], [stage])
                    p.tt("v", so, so, pT.h[:, 1024:1024 + 512], A.add,
                         [stage, pT1], [stage])
                    tok = p.dma("y", out_h[ob + tqb: ob + tqb + 128,
                                           dh * 512:(dh + 1) * 512], so,
                                [stage], [NONE])
                    store_toks.append(tok)

    for tok in store_toks:
        p._wait("y", tok)

    p.es.close()
    return nc


def _sel_const():
    sel = np.zeros((64, NH * Q), np.float32)
    for h in range(NH):
        sel[h, h * Q:(h + 1) * Q] = 1.0
        sel[NH + h, h * Q:(h + 1) * Q] = 1.0
    return sel.astype(ml_dtypes.bfloat16)


def kernel(**inputs):
    ins = {k: np.asarray(v) for k, v in inputs.items()}
    x = np.ascontiguousarray(ins["L0_attn_x"], dtype=np.float32)
    def _pp(a, nc_):
        a = np.asarray(a, np.float32)
        return np.ascontiguousarray(np.moveaxis(a.reshape(*a.shape[:-1], nc_, 128), -1, -2))
    conv_w = np.ascontiguousarray(
        np.transpose(ins["L0_attn_conv_w"], (0, 2, 1)).reshape(NL, XC, 128, 4)
        .transpose(0, 2, 1, 3).reshape(NL, 128, XC * 4), dtype=np.float32)
    common = {
        "in_w": np.ascontiguousarray(ins["L0_attn_in_w"], np.float32),
        "out_w": np.ascontiguousarray(ins["L0_attn_out_w"], np.float32),
        "conv_w": conv_w,
        "conv_b": _pp(ins["L0_attn_conv_b"], XC),
        "ln_w": _pp(ins["L0_attn_ln_w"], 8),
        "ln_b": _pp(ins["L0_attn_ln_b"], 8),
        "dt_bias": np.ascontiguousarray(ins["L0_attn_dt_bias"], np.float32),
        "A_log": np.ascontiguousarray(ins["L0_attn_A_log"], np.float32),
        "Dp": np.ascontiguousarray(ins["L0_attn_Dp"], np.float32).reshape(NL, 1, NH),
        "norm_w": _pp(ins["L0_attn_norm_w"], ZC),
        "w1": np.ascontiguousarray(ins["L0_ffn_w1"], np.float32),
        "b1": _pp(ins["L0_ffn_b1"], 32),
        "w2": np.ascontiguousarray(ins["L0_ffn_w2"], np.float32),
        "b2": np.ascontiguousarray(ins["L0_ffn_b2"], np.float32).reshape(1, D),
        "ln1_w": _pp(ins["L0_ffn_ln1_w"], 8),
        "ln1_b": _pp(ins["L0_ffn_ln1_b"], 8),
        "ln2_w": _pp(ins["L0_ffn_ln2_w"], 32),
        "ln2_b": _pp(ins["L0_ffn_ln2_b"], 32),
        "c_ident": np.eye(128, dtype=np.float32).astype(ml_dtypes.bfloat16),
        "c_mask": np.triu(np.ones((128, 128), np.float32)).astype(ml_dtypes.bfloat16),
        "c_sel": _sel_const(),
        "c_mneg": np.tile(np.where(np.triu(np.ones((128, 128), np.float32)) > 0, 0.0, -1e9).astype(np.float32), (1, 4)).astype(ml_dtypes.bfloat16),
        "c_orow": np.ones((1, 128), np.float32).astype(ml_dtypes.bfloat16),
        "c_ocol": np.ones((128, 1), np.float32).astype(ml_dtypes.bfloat16),
        "c_oc32": np.ones((128, 1), np.float32),
        "c_or32": np.ones((1, 128), np.float32),
        "c_o32": np.ones((NH, Q), np.float32),
    }
    nc = build()
    in_maps = [dict(common, x=np.ascontiguousarray(x[b])) for b in range(B)]
    res = run_bass_kernel_spmd(nc, in_maps, core_ids=list(range(8)))
    out = np.stack([res.results[b]["out"] for b in range(B)], axis=0)
    return out.astype(np.float32)


# version 22 (final)
# speedup vs baseline: 1.0372x; 1.0372x over previous
import contextlib
import numpy as np
import ml_dtypes
import concourse.bass as bass
import concourse.mybir as mybir
from concourse.alu_op_type import AluOpType as A
from concourse.bass_isa import ReduceOp
import concourse.library_config as libcfg
from concourse.bass_utils import run_bass_kernel_spmd

F32 = mybir.dt.float32
BF16 = mybir.dt.bfloat16
I32 = mybir.dt.int32
AF = mybir.ActivationFunctionType
AX = mybir.AxisListType.X

B = 8
L, D, NL, DST, DIN, NH = 4096, 1024, 4, 64, 2048, 32
CONV_DIM = DIN + 2 * DST
EIN = 2 * DIN + 2 * DST + NH
HID = 4096
EPS = 1e-5
OT, TT, Q = 1024, 512, 128
NO, NI, NQ = L // OT, OT // TT, TT // Q
ZC, XC = DIN // 128, CONV_DIM // 128
EC = 34
MAGIC = 12582912.0
CAP = 1e30
CIW = TT + 3

_DBG = None


class Tk:
    __slots__ = ("h", "w", "r", "name")

    def __init__(self, h, name=""):
        self.h, self.name, self.w, self.r = h, name, None, {}


NONE = Tk(None)


class Prog:
    def __init__(self, nc):
        self.nc = nc
        self.es = contextlib.ExitStack()
        self.eng = {"t": nc.tensor, "v": nc.vector, "s": nc.scalar,
                    "g": nc.gpsimd, "y": nc.sync}
        self.sem, self.pool, self.pool_i, self.pool_cnt = {}, {}, {}, {}
        self.cnt = {e: 0 for e in "tvsgy"}
        self.mark = {e: {} for e in "tvsgy"}
        self.drained = {e: 0 for e in "tvsgy"}
        self.dirty = {"g": set(), "y": set()}

    def setup(self, npool=48):
        for e in "tvsg":
            self.sem[e] = self.es.enter_context(self.nc.semaphore(f"p_{e}"))
        for qn in ("g", "y"):
            self.pool[qn] = [self.es.enter_context(self.nc.semaphore(f"d{qn}{i}"))
                             for i in range(npool)]
            self.pool_i[qn] = 0
            self.pool_cnt[qn] = [0] * npool

    def sb(self, name, shape, dt):
        return Tk(self.es.enter_context(self.nc.sbuf_tensor(name, shape, dt)), name)

    def ps(self, name, shape, dt):
        return Tk(self.es.enter_context(self.nc.psum_tensor(name, shape, dt)), name)

    def region(self, tk, name):
        return Tk(tk.h, name)

    def snapshot(self):
        toks = {}
        for e in "tvsg":
            if self.cnt[e]:
                toks[("snap", e)] = ("c", e, self.cnt[e])
        for qn in ("g", "y"):
            for idx in self.dirty[qn]:
                toks[("snapd", qn, idx)] = ("d", qn, idx, self.pool_cnt[qn][idx])
            self.dirty[qn] = set()
        return toks

    def _wait(self, e, tok):
        if tok[0] == "c":
            _, oe, v = tok
            if oe == e:
                if v > self.drained[e]:
                    self.eng[e].drain()
                    self.drained[e] = self.cnt[e]
            else:
                if self.mark[e].get(oe, 0) < v:
                    self.eng[e].wait_ge(self.sem[oe], v)
                    self.mark[e][oe] = v
        else:
            _, qn, idx, v = tok
            key = (qn, idx)
            if self.mark[e].get(key, 0) < v:
                self.eng[e].wait_ge(self.pool[qn][idx], v)
                self.mark[e][key] = v

    def dep(self, e, ins, outs):
        for t in ins:
            if t is not None and t.w is not None:
                self._wait(e, t.w)
        for t in outs:
            if t is None:
                continue
            if t.w is not None:
                self._wait(e, t.w)
            for tok in list(t.r.values()):
                self._wait(e, tok)

    warm_ap = None
    warm_every = 0

    def fin(self, e, instr, ins, outs):
        if e == "v" and self.warm_every and self.cnt["v"] % self.warm_every == 0:
            if self.mark["t"].get("v", 0) < self.cnt["v"]:
                self.eng["t"].wait_ge(self.sem["v"], self.cnt["v"])
                self.mark["t"]["v"] = self.cnt["v"]
            self.eng["t"].ldweights(self.warm_ap)
        self.cnt[e] += 1
        tok = ("c", e, self.cnt[e])
        instr.then_inc(self.sem[e], 1)
        for t in ins:
            if t is not None:
                t.r[("c", e)] = tok
        for t in outs:
            if t is not None:
                t.w, t.r = tok, {}
        return tok

    def tt(self, e, oap, a0, a1, op, ins, outs):
        self.dep(e, ins, outs)
        return self.fin(e, self.eng[e].tensor_tensor(out=oap, in0=a0, in1=a1, op=op),
                        ins, outs)

    def ts(self, e, oap, a0, s1, s2, op0, op1, ins, outs):
        self.dep(e, ins, outs)
        kw = dict(out=oap, in0=a0, scalar1=s1, scalar2=s2, op0=op0)
        if op1 is not None:
            kw["op1"] = op1
        return self.fin(e, self.eng[e].tensor_scalar(**kw), ins, outs)

    def stt(self, e, oap, a0, scalar, a1, op0, op1, ins, outs):
        self.dep(e, ins, outs)
        return self.fin(e, self.eng[e].scalar_tensor_tensor(
            out=oap, in0=a0, scalar=scalar, in1=a1, op0=op0, op1=op1), ins, outs)

    def cp(self, e, oap, a0, ins, outs):
        self.dep(e, ins, outs)
        if e == "s":
            return self.fin(e, self.eng[e].copy(out=oap, in_=a0), ins, outs)
        return self.fin(e, self.eng[e].tensor_copy(out=oap, in_=a0), ins, outs)

    def act(self, e, oap, a0, func, ins, outs, bias=0.0, scale=1.0):
        self.dep(e, ins, outs)
        if isinstance(bias, float):
            bias = np.float32(bias)
        if isinstance(scale, float):
            scale = np.float32(scale)
        return self.fin(e, self.eng[e].activation(out=oap, in_=a0, func=func,
                                                  bias=bias, scale=scale), ins, outs)

    def red(self, e, oap, a0, op, ins, outs, absval=False):
        self.dep(e, ins, outs)
        f = self.eng[e].reduce_max if op == "max" else self.eng[e].reduce_sum
        return self.fin(e, f(out=oap, in_=a0, axis=AX, apply_absolute_value=absval),
                        ins, outs)

    def recip(self, e, oap, a0, ins, outs):
        self.dep(e, ins, outs)
        return self.fin(e, self.eng[e].reciprocal(out=oap, in_=a0), ins, outs)

    def scan(self, e, oap, d0, d1, initial, op0, op1, ins, outs):
        self.dep(e, ins, outs)
        return self.fin(e, self.eng[e].tensor_tensor_scan(
            out=oap, data0=d0, data1=d1, initial=initial, op0=op0, op1=op1), ins, outs)

    def memset(self, e, oap, val, outs):
        self.dep(e, [], outs)
        return self.fin(e, self.eng[e].memset(oap, val), [], outs)

    def mmg(self, mms, ins, outs):
        self.dep("t", ins, outs)
        last = None
        for m in mms:
            oap, lap, rap, st, sp = m[:5]
            tp = m[5] if len(m) > 5 else None
            last = self.eng["t"].matmul(oap, lap, rap, start=st, stop=sp,
                                        tile_position=tp)
        return self.fin("t", last, ins, outs)

    def trg(self, trs, ins, outs):
        self.dep("t", ins, outs)
        last = None
        for (oap, iap, ident) in trs:
            last = self.eng["t"].transpose(oap, iap, ident)
        return self.fin("t", last, ins, outs)

    def dma(self, qn, oap, a0, ins, outs):
        self.dep(qn, ins, outs)
        idx = self.pool_i[qn]
        prev = self.pool_cnt[qn][idx]
        if prev > 0 and self.mark[qn].get((qn, idx), 0) < prev:
            self.eng[qn].wait_ge(self.pool[qn][idx], prev)
            self.mark[qn][(qn, idx)] = prev
        i = self.eng[qn].dma_start(out=oap, in_=a0)
        self.pool_i[qn] = (idx + 1) % len(self.pool[qn])
        self.pool_cnt[qn][idx] += 16
        self.dirty[qn].add(idx)
        tok = ("d", qn, idx, self.pool_cnt[qn][idx])
        i.then_inc(self.pool[qn][idx], 16)
        for t in ins:
            if t is not None:
                t.r[("q", qn, idx)] = tok
        for t in outs:
            if t is not None:
                t.w, t.r = tok, {}
        return tok


class Carver:

    def __init__(self, p, cols_bf16):
        self.p = p
        self.t = p.es.enter_context(p.nc.sbuf_tensor("arena", [128, cols_bf16], BF16))
        self.cols = cols_bf16
        self.off = 0
        self.frontier = {}

    def newgen(self):
        self.off = 0
        self.frontier = self.p.snapshot()

    def get(self, name, shape, dt):
        assert shape[0] <= 128
        esz = {F32: 4, BF16: 2, I32: 4}[dt]
        ncols = shape[1]
        nb = ncols * esz
        nb = (nb + 127) // 128 * 128
        start_b = self.off
        self.off += nb
        assert self.off <= self.cols * 2, f"arena overflow at {name}: {self.off}"
        base = self.t[:].bitcast(dt) if dt != BF16 else self.t[:]
        view = base[0:shape[0], start_b // esz: start_b // esz + ncols]
        tk = Tk(view, name)
        tk.r = dict(self.frontier)
        return tk


def svw(h, base, stride, n, w):
    return h[:, base:base + n * stride].rearrange("p (n s) -> p n s", n=n)[:, :, 0:w]


def bcv(ap2d, n):
    return ap2d.unsqueeze(1).broadcast_to([ap2d.shape[0], n, ap2d.shape[1]])


def build():
    nc = bass.Bass("TRN2", target_bir_lowering=False)
    p = Prog(nc)

    x_h = nc.dram_tensor("x", [L, D], F32, kind="ExternalInput").ap()
    out_h = nc.dram_tensor("out", [L, D], F32, kind="ExternalOutput").ap()
    in_w_h = nc.dram_tensor("in_w", [NL, D, EIN], F32, kind="ExternalInput").ap()
    out_w_h = nc.dram_tensor("out_w", [NL, DIN, D], F32, kind="ExternalInput").ap()
    convw_h = nc.dram_tensor("conv_w", [NL, 128, XC * 4], F32, kind="ExternalInput").ap()
    convb_h = nc.dram_tensor("conv_b", [NL, 128, XC], F32, kind="ExternalInput").ap()
    lnw_h = nc.dram_tensor("ln_w", [NL, 128, 8], F32, kind="ExternalInput").ap()
    lnb_h = nc.dram_tensor("ln_b", [NL, 128, 8], F32, kind="ExternalInput").ap()
    dtb_h = nc.dram_tensor("dt_bias", [NL, NH], F32, kind="ExternalInput").ap()
    alog_h = nc.dram_tensor("A_log", [NL, NH], F32, kind="ExternalInput").ap()
    dp_h = nc.dram_tensor("Dp", [NL, 1, NH], F32, kind="ExternalInput").ap()
    nw_h = nc.dram_tensor("norm_w", [NL, 128, ZC], F32, kind="ExternalInput").ap()
    w1_h = nc.dram_tensor("w1", [HID, D], F32, kind="ExternalInput").ap()
    b1_h = nc.dram_tensor("b1", [128, 32], F32, kind="ExternalInput").ap()
    w2_h = nc.dram_tensor("w2", [D, HID], F32, kind="ExternalInput").ap()
    b2_h = nc.dram_tensor("b2", [1, D], F32, kind="ExternalInput").ap()
    ln1w_h = nc.dram_tensor("ln1_w", [128, 8], F32, kind="ExternalInput").ap()
    ln1b_h = nc.dram_tensor("ln1_b", [128, 8], F32, kind="ExternalInput").ap()
    ln2w_h = nc.dram_tensor("ln2_w", [128, 32], F32, kind="ExternalInput").ap()
    ln2b_h = nc.dram_tensor("ln2_b", [128, 32], F32, kind="ExternalInput").ap()
    ident_h = nc.dram_tensor("c_ident", [128, 128], BF16, kind="ExternalInput").ap()
    mask_h = nc.dram_tensor("c_mask", [128, 128], BF16, kind="ExternalInput").ap()
    sel_h = nc.dram_tensor("c_sel", [64, NH * Q], BF16, kind="ExternalInput").ap()
    mneg_h = nc.dram_tensor("c_mneg", [128, 512], BF16, kind="ExternalInput").ap()
    orow_h = nc.dram_tensor("c_orow", [1, 128], BF16, kind="ExternalInput").ap()
    ocol_h = nc.dram_tensor("c_ocol", [128, 1], BF16, kind="ExternalInput").ap()
    oc32_h = nc.dram_tensor("c_oc32", [128, 1], F32, kind="ExternalInput").ap()
    or32_h = nc.dram_tensor("c_or32", [1, 128], F32, kind="ExternalInput").ap()
    o32_h = nc.dram_tensor("c_o32", [NH, Q], F32, kind="ExternalInput").ap()
    qw1_s = nc.dram_tensor("qw1_s", [D, HID], BF16, kind="Internal").ap()
    qw2_s = nc.dram_tensor("qw2_s", [HID, D], BF16, kind="Internal").ap()
    inw_s = nc.dram_tensor("inw_s", [NL, D, EIN], BF16, kind="Internal").ap()
    outw_s = nc.dram_tensor("outw_s", [NL, DIN, D], BF16, kind="Internal").ap()

    dbg = {}
    if _DBG:
        for name, shape, dt in _DBG:
            dbg[name] = nc.dram_tensor("dbg_" + name, shape, dt,
                                       kind="ExternalOutput").ap()

    p.setup()
    sb, ps, rg = p.sb, p.ps, p.region

    xseg = sb("xseg", [128, 8 * OT + OT], BF16)
    wbig = sb("wbig", [128, 8 * 2208], BF16)
    wbA = Tk(wbig.h[:, 0:4 * 2208], "wbA")
    wbB = Tk(wbig.h[:, 4 * 2208:8 * 2208], "wbB")
    hst = [sb(f"hst{l}", [64, DIN], BF16) for l in range(NL)]
    hro = [0 for l in range(NL)]
    halo = [sb(f"halo{l}", [128, XC * 3], BF16) for l in range(NL)]
    convw = [sb(f"convw{l}", [128, XC * 4], F32) for l in range(NL)]
    convbb = [sb(f"convb{l}", [128, XC], F32) for l in range(NL)]
    lnw = [sb(f"lnw{l}", [128, 8], F32) for l in range(NL)]
    lnb = [sb(f"lnb{l}", [128, 8], F32) for l in range(NL)]
    dtb = [sb(f"dtb{l}", [NH, 1], F32) for l in range(NL)]
    nA = [sb(f"nA{l}", [NH, 1], F32) for l in range(NL)]
    dprow = [sb(f"dprow{l}", [1, NH], BF16) for l in range(NL)]
    nwv = [sb(f"nwv{l}", [128, ZC], F32) for l in range(NL)]
    stage = sb("stage", [128, D], F32)
    stg16 = sb("stg16", [128, D], BF16)
    sq = sb("sq", [128, TT], BF16)
    sq2 = sb("sq2", [128, TT], BF16)
    srow = sb("srow", [1, TT], F32)
    mrow = sb("mrow", [1, TT], F32)
    rrow = sb("rrow", [1, TT], F32)
    brow = sb("brow", [1, 2 * TT], BF16)
    bcast = sb("bcast", [128, 2 * TT], BF16)
    mu128 = sb("mu128", [128, 1], F32)
    sd128 = sb("sd128", [128, 1], F32)
    v128 = sb("v128", [128, 1], F32)
    wqs = sb("wqs", [128, 1], F32)
    wqr = sb("wqr", [1, 1], F32)
    wiv = [sb(f"wiv{i}", [1, 1], F32) for i in range(2)]
    wq16 = sb("wq16", [128, 1], BF16)
    b1c = sb("b1c", [128, 32], F32)
    b2row = sb("b2row", [1, D], BF16)
    b2bc = sb("b2bc", [128, D], BF16)
    invc = sb("invc", [128, 4], F32)
    g1c = sb("g1c", [128, 8], F32)
    e1c = sb("e1c", [128, 8], F32)
    g2c = sb("g2c", [128, 32], F32)
    e2c = sb("e2c", [128, 32], F32)
    ident = sb("ident", [128, 128], BF16)
    mask01 = sb("mask01", [128, 128], BF16)
    mneg = sb("mneg", [128, 512], BF16)
    orow = sb("orow", [1, 128], BF16)
    ocol = sb("ocol", [128, 1], BF16)
    oc32 = sb("oc32", [128, 1], F32)
    or32 = sb("or32", [1, 128], F32)
    wqr128 = sb("wqr128", [128, 1], F32)
    wiv128 = [sb(f"wiv128_{i}", [128, 1], F32) for i in range(2)]
    mq4 = sb("mq4", [128, 4], F32)
    mt4 = sb("mt4", [128, 4], F32)
    sc8 = sb("sc8", [128, 8], BF16)
    sr8 = sb("sr8", [8, 128], BF16)
    o32 = sb("o32", [NH, Q], F32)
    selcat = sb("selcat", [66, NH * Q], BF16)
    ldtcat = sb("ldtcat", [66, 2 * Q], BF16)

    arena = Carver(p, 52160)

    pD0 = ps("pD0", [128, 1024], F32)
    pD1 = ps("pD1", [128, 1024], F32)
    pY = ps("pY", [128, 1024], F32)
    pT = ps("pT", [128, 2048], BF16)
    pD0L, pD0R = rg(pD0, "pD0L"), rg(pD0, "pD0R")
    pD1L, pD1R = rg(pD1, "pD1L"), rg(pD1, "pD1R")
    pYL, pYR = rg(pY, "pYL"), rg(pY, "pYR")
    pT0, pT1 = rg(pT, "pT0"), rg(pT, "pT1")
    slot_t = [pD0L, pD0R, pD1L, pD1R, pYL, pYR]
    slot_a = [pD0.h[:, 0:512], pD0.h[:, 512:1024], pD1.h[:, 0:512],
              pD1.h[:, 512:1024], pY.h[:, 0:512], pY.h[:, 512:1024]]

    p.dma("y", ident.h[:], ident_h, [], [ident])
    p.warm_ap = ident.h[:, 0:2]
    p.dma("y", mask01.h[:], mask_h, [], [mask01])
    p.dma("y", mneg.h[:], mneg_h, [], [mneg])
    p.dma("y", orow.h[:], orow_h, [], [orow])
    p.dma("y", ocol.h[:], ocol_h, [], [ocol])
    p.dma("y", oc32.h[:], oc32_h, [], [oc32])
    p.dma("y", or32.h[:], or32_h, [], [or32])
    p.dma("y", o32.h[:], o32_h, [], [o32])
    p.dma("y", selcat.h[0:64, :], sel_h, [], [selcat])
    p.memset("v", ldtcat.h[64:66, :], 1.0, [ldtcat])
    for l in range(NL):
        p.dma("y", convbb[l].h[:], convb_h[l], [], [convbb[l]])
        p.dma("y", convw[l].h[:], convw_h[l], [], [convw[l]])
        p.dma("y", lnw[l].h[:], lnw_h[l], [], [lnw[l]])
        p.dma("y", lnb[l].h[:], lnb_h[l], [], [lnb[l]])
        p.dma("y", dtb[l].h[:], dtb_h[l].unsqueeze(1), [], [dtb[l]])
        p.dma("y", nA[l].h[:], alog_h[l].unsqueeze(1), [], [nA[l]])
        p.dma("g", dprow[l].h[:], dp_h[l], [], [dprow[l]])
        p.dma("y", nwv[l].h[:], nw_h[l], [], [nwv[l]])
        p.act("s", nA[l].h[:], nA[l].h[:], AF.Exp, [nA[l]], [nA[l]])
        p.ts("v", nA[l].h[:], nA[l].h[:], -1.0, None, A.mult, None, [nA[l]], [nA[l]])
        p.memset("v", hst[l].h[hro[l]:hro[l] + 64, :], 0.0, [hst[l]])
        p.memset("v", halo[l].h[:], 0.0, [halo[l]])
    p.dma("y", b1c.h[:], b1_h, [], [b1c])
    p.dma("g", b2row.h[:], b2_h, [], [b2row])
    p.mmg([(pD0.h[:, 0:512], orow.h[0:1, :], b2row.h[0:1, 0:512], True, True)],
          [b2row, orow], [pD0L])
    p.mmg([(pD0.h[:, 512:1024], orow.h[0:1, :], b2row.h[0:1, 512:1024], True, True)],
          [b2row, orow], [pD0R])
    p.cp("v", b2bc.h[:], pD0.h[:, 0:1024], [pD0L, pD0R], [b2bc])
    p.dma("y", g1c.h[:], ln1w_h, [], [g1c])
    p.dma("y", e1c.h[:], ln1b_h, [], [e1c])
    p.dma("y", g2c.h[:], ln2w_h, [], [g2c])
    p.dma("y", e2c.h[:], ln2b_h, [], [e2c])

    def phase0(o):
        ob = o * OT
        for tti in range(OT // 128):
            tb = tti * 128
            p.dma("y", stage.h[:], x_h[ob + tb: ob + tb + 128, :], [], [stage])
            p.red("v", mu128.h[:], stage.h[:], "sum", [stage], [mu128])
            p.act("s", stg16.h[:], stage.h[:], AF.Square, [stage], [stg16])
            p.red("v", sd128.h[:], stg16.h[:], "sum", [stg16], [sd128])
            p.ts("v", mu128.h[:], mu128.h[:], 1.0 / D, None, A.mult, None,
                 [mu128], [mu128])
            p.ts("v", sd128.h[:], sd128.h[:], 1.0 / D, None, A.mult, None,
                 [sd128], [sd128])
            p.tt("v", v128.h[:], mu128.h[:], mu128.h[:], A.mult, [mu128], [v128])
            p.tt("v", sd128.h[:], sd128.h[:], v128.h[:], A.subtract,
                 [sd128, v128], [sd128])
            p.act("s", sd128.h[:], sd128.h[:], AF.Ln, [sd128], [sd128], bias=EPS)
            p.act("s", sd128.h[:], sd128.h[:], AF.Exp, [sd128], [sd128], scale=-0.5)
            p.ts("v", stg16.h[:], stage.h[:], mu128.h[:], sd128.h[:],
                 A.subtract, A.mult, [stage, mu128, sd128], [stg16])
            trs = [(pT.h[:, j * 128:(j + 1) * 128], stg16.h[:, j * 128:(j + 1) * 128],
                    ident.h[:]) for j in range(8)]
            p.trg(trs, [stg16, ident], [pT0, pT1])
            p.cp("v", svw(xseg.h, tb, OT, 8, 128),
                 pT.h[:, 0:1024].rearrange("p (n w) -> p n w", n=8),
                 [pT0, pT1], [xseg])


    phase0(0)

    _bcnt = 0
    for l in range(NL):
        for k in range(8):
            for wv, (c0, c1) in enumerate(((0, 2048), (2048, EIN))):
                bo, bt = (0, wbA) if _bcnt % 2 == 0 else (4 * 2208, wbB)
                _bcnt += 1
                p.dma("g", wbig.h[:, bo:bo + c1 - c0],
                      in_w_h[l, k * 128:(k + 1) * 128, c0:c1], [], [bt])
                p.dma("y", inw_s[l, k * 128:(k + 1) * 128, c0:c1],
                      wbig.h[:, bo:bo + c1 - c0], [bt], [NONE])
        for k in range(16):
            bo, bt = (0, wbA) if _bcnt % 2 == 0 else (4 * 2208, wbB)
            _bcnt += 1
            p.dma("g", wbig.h[:, bo:bo + 1024],
                  out_w_h[l, k * 128:(k + 1) * 128, :], [], [bt])
            p.dma("y", outw_s[l, k * 128:(k + 1) * 128, :],
                  wbig.h[:, bo:bo + 1024], [bt], [NONE])

    def quant_weights(w_src, n_rows, n_cols, scratch, wiv_t, wiv128_t):
        nrc = n_rows // 128
        first = True
        for k in range(nrc):
            for cc in range(n_cols // 1024):
                p.dma("y", stage.h[:], w_src[k * 128:(k + 1) * 128,
                                             cc * 1024:(cc + 1) * 1024], [], [stage])
                p.red("v", v128.h[:], stage.h[:], "sum", [stage], [v128], absval=True)
                if first:
                    p.cp("v", wqs.h[:], v128.h[:], [v128], [wqs])
                    first = False
                else:
                    p.tt("v", wqs.h[:], wqs.h[:], v128.h[:], A.add, [wqs, v128], [wqs])
        p.mmg([(pY.h[0:1, 0:1], oc32.h[:], wqs.h[:], True, True)], [wqs, oc32], [pYL])
        p.cp("v", wqr.h[:], pY.h[0:1, 0:1], [pYL], [wqr])
        p.ts("v", wiv_t.h[:], wqr.h[:], 1.0 / (n_rows * n_cols), EPS,
             A.mult, A.max, [wqr], [wiv_t])
        p.mmg([(pY.h[:, 0:1], or32.h[:], wqr.h[:], True, True)], [wqr, or32], [pYL])
        p.ts("v", wiv128_t.h[:], pY.h[:, 0:1], 1.0 / (n_rows * n_cols), EPS,
             A.mult, A.max, [pYL], [wiv128_t])
        p.cp("v", wqr128.h[:], wiv128_t.h[:], [wiv128_t], [wqr128])
        p.recip("v", wqr128.h[:], wqr128.h[:], [wqr128], [wqr128])
        ivw = stage.h[:].bitcast(I32)[:, 0:1024]
        for k in range(nrc):
            for cc in range(n_cols // 1024):
                p.dma("y", stage.h[:], w_src[k * 128:(k + 1) * 128,
                                             cc * 1024:(cc + 1) * 1024], [], [stage])
                p.ts("v", stage.h[:], stage.h[:], wqr128.h[:], -1.49, A.mult, A.max,
                     [stage, wqr128], [stage])
                p.ts("v", stage.h[:], stage.h[:], 1.49, MAGIC, A.min, A.add,
                     [stage], [stage])
                p.ts("v", ivw, stage.h[:], MAGIC - 1024.0, 0.25, A.subtract, A.add,
                     [stage], [stage])
                p.ts("v", stg16.h[:, 0:1024], ivw, 1024.0, None, A.subtract, None,
                     [stage], [stg16])
                trs = [(pT.h[:, j * 128:(j + 1) * 128],
                        stg16.h[:, j * 128:(j + 1) * 128], ident.h[:])
                       for j in range(8)]
                p.trg(trs, [stg16, ident], [pT0, pT1])
                p.cp("v", bcast.h[:, 0:1024], pT.h[:, 0:1024], [pT0, pT1], [bcast])
                p.dma("y",
                      scratch[cc * 1024:(cc + 1) * 1024, k * 128:(k + 1) * 128]
                      .rearrange("(a p) b -> p a b", p=128),
                      bcast.h[:, 0:1024].rearrange("p (a b) -> p a b", a=8),
                      [bcast], [NONE])

    _quant_todo = [(w1_h, HID, D, qw1_s, wiv[0], wiv128[0]),
                   (w2_h, D, HID, qw2_s, wiv[1], wiv128[1])]

    store_toks = []

    def fm_stats(src_t, apfn, nf):
        mms = [(pY.h[0:1, 0:TT], ocol.h[:], apfn(j), j == 0, j == nf - 1)
               for j in range(nf)]
        p.mmg(mms, [src_t, ocol], [pYL])
        bufs = (sq, sq2)
        for j in range(nf):
            b = bufs[j % 2]
            p.act("s", b.h[:, 0:TT], apfn(j), AF.Square, [src_t], [b])
            p.mmg([(pY.h[32:33, 512:512 + TT], ocol.h[:], b.h[:, 0:TT],
                    j == 0, j == nf - 1)], [b, ocol], [pYR])

    def rstd_mu(nfe):
        p.ts("v", mrow.h[:], pY.h[0:1, 0:TT], 1.0 / nfe, None, A.mult, None,
             [pYL], [mrow])
        p.ts("v", srow.h[:], pY.h[32:33, 512:512 + TT], 1.0 / nfe, None, A.mult,
             None, [pYR], [srow])
        p.tt("v", rrow.h[:], mrow.h[:], mrow.h[:], A.mult, [mrow], [rrow])
        p.tt("v", srow.h[:], srow.h[:], rrow.h[:], A.subtract, [srow, rrow], [srow])
        p.act("s", rrow.h[:], srow.h[:], AF.Ln, [srow], [rrow], bias=EPS)
        p.act("s", rrow.h[:], rrow.h[:], AF.Exp, [rrow], [rrow], scale=-0.5)

    def bcast_rows(r0_t, r0_ap, r1_t, r1_ap):
        p.cp("v", brow.h[0:1, 0:TT], r0_ap, [r0_t], [brow])
        p.cp("v", brow.h[0:1, TT:2 * TT], r1_ap, [r1_t], [brow])
        p.mmg([(pD1.h[:, 0:512], orow.h[0:1, :], brow.h[0:1, 0:512], True, True)],
              [brow, orow], [pD1L])
        p.mmg([(pD1.h[:, 512:1024], orow.h[0:1, :], brow.h[0:1, 512:1024], True, True)],
              [brow, orow], [pD1R])
        p.cp("v", bcast.h[:, 0:1024], pD1.h[:, 0:1024], [pD1L, pD1R], [bcast])

    def apply_ln(src_t, src_view, out_t, out_view, nf, gam, bet):
        p.tt("v", out_view, src_view, bcv(bcast.h[:, TT:2 * TT], nf), A.subtract,
             [src_t, bcast], [out_t])
        p.tt("v", out_view, out_view, bcv(bcast.h[:, 0:TT], nf), A.mult,
             [out_t, bcast], [out_t])
        if gam is not None:
            for j in range(nf):
                osl = out_view[:, j, :] if False else None
            return

    _sqm_holder = {}

    def qchain(xt, nf, wiv_t, wiv128_t):
        sqm = _sqm_holder["t"]
        """Act-quant per token over nf*128 features (feature-major), in place.
        bcast[:,0:TT] <- 127/max (quant scale), bcast[:,TT:2TT] <- max/127*wiv."""
        sqm2 = [rg(sqm, "sqmP"), rg(sqm, "sqmN")]
        for j in range(nf):
            a = xt.h[:, j * TT:(j + 1) * TT]
            for sl_i, base_op in ((0, A.max), (1, A.min)):
                r = sqm.h[:, sl_i * TT:(sl_i + 1) * TT]
                if j == 0:
                    p.cp("v", r, a, [xt], [sqm2[sl_i]])
                else:
                    p.tt("v", r, r, a, base_op, [xt, sqm2[sl_i]], [sqm2[sl_i]])
        p.ts("v", sqm.h[:, TT:2 * TT], sqm.h[:, TT:2 * TT], -1.0, None,
             A.mult, None, [sqm2[1]], [sqm2[1]])
        p.tt("v", sq.h[:], sqm.h[:, 0:TT], sqm.h[:, TT:2 * TT], A.max,
             [sqm2[0], sqm2[1]], [sq])
        trs = [(pT.h[:, 128 * tq:128 * (tq + 1)], sq.h[:, tq * 128:(tq + 1) * 128],
                ident.h[:]) for tq in range(4)]
        p.trg(trs, [sq, ident], [pT0])
        for tq in range(4):
            p.red("v", mq4.h[:, tq:tq + 1], pT.h[:, 128 * tq:128 * (tq + 1)],
                  "max", [pT0], [mq4])
        p.ts("v", mq4.h[:], mq4.h[:], EPS, None, A.max, None, [mq4], [mq4])
        p.act("s", mt4.h[:], mq4.h[:], AF.Ln, [mq4], [mt4])
        p.act("s", mt4.h[:], mt4.h[:], AF.Exp, [mt4], [mt4], scale=-1.0,
              bias=4.844187086)
        p.cp("v", sc8.h[:, 0:4], mt4.h[:], [mt4], [sc8])
        p.ts("v", mq4.h[:], mq4.h[:], wiv128_t.h[:], 1.0 / 127.0, A.mult, A.mult,
             [mq4, wiv128_t], [mq4])
        p.cp("v", sc8.h[:, 4:8], mq4.h[:], [mq4], [sc8])
        p.trg([(pT.h[0:8, 64:64 + 128], sc8.h[:], ident.h[:])], [sc8, ident], [pT0])
        p.cp("v", sr8.h[:], pT.h[0:8, 64:64 + 128], [pT0], [sr8])
        p.dma("y", brow.h[0:1, :], sr8.h[:], [sr8], [brow])
        p.mmg([(pD1.h[:, 0:512], orow.h[0:1, :], brow.h[0:1, 0:512], True, True)],
              [brow, orow], [pD1L])
        p.mmg([(pD1.h[:, 512:1024], orow.h[0:1, :], brow.h[0:1, 512:1024],
                True, True)], [brow, orow], [pD1R])
        p.cp("v", bcast.h[:, 0:1024], pD1.h[:, 0:1024], [pD1L, pD1R], [bcast])
        for s0 in range(0, nf, 4):
            ns = min(4, nf - s0)
            xv = svw(xt.h, s0 * TT, TT, ns, TT)
            fv = fq.h[:, 0:ns * TT].rearrange("p (n w) -> p n w", n=ns)
            p.tt("v", fv, xv, bcv(bcast.h[:, 0:TT], ns), A.mult, [xt, bcast], [fq])
            fa = fq.h[:, 0:ns * TT]
            p.ts("v", fa, fa, -128.49, 127.49, A.max, A.min, [fq], [fq])
            p.ts("v", fa, fa, MAGIC, MAGIC - 1024.0, A.add, A.subtract, [fq], [fq])
            p.ts("v", iqt.h[:, 0:ns * TT], fa, 0.25, None, A.add, None, [fq], [iqt])
            p.ts("v", xt.h[:, s0 * TT:(s0 + ns) * TT], iqt.h[:, 0:ns * TT], 1024.0,
                 None, A.subtract, None, [iqt], [xt])

    for o in range(NO):
        ob = o * OT
        arena.newgen()
        xn = arena.get("xn", [128, 8 * TT], BF16)
        xnH = []
        for h_ in range(2):
            t = Tk(xn.h[:, h_ * 4 * TT:(h_ * 4 + 4) * TT], f"xn{h_}")
            t.r = dict(xn.r)
            xnH.append(t)
        zt = arena.get("zt", [128, ZC * TT + TT], BF16)
        convin = arena.get("convin", [128, XC * CIW + 8], BF16)
        cvinJ = []
        for j in range(XC):
            ce = (j + 1) * CIW + (8 if j == XC - 1 else 0)
            t = Tk(convin.h[:, j * CIW:ce], f"cvin{j}")
            t.r = dict(convin.r)
            cvinJ.append(t)
        convout = arena.get("convout", [128, XC * TT], BF16)
        cvJ = []
        for j in range(XC):
            t = Tk(convout.h[:, j * TT:(j + 1) * TT], f"cv{j}")
            t.r = dict(convout.r)
            cvJ.append(t)
        dpi = arena.get("dpi", [128, NH * Q], BF16)
        stok = arena.get("stok", [128, NH * Q // 2], BF16)
        ssta = arena.get("ssta", [64, NH * Q // 2], BF16)
        xsT = arena.get("xsT", [128, DIN], BF16)
        xbar = arena.get("xbar", [128, DIN], BF16)
        dtraw = arena.get("dtraw", [NH, TT], F32)
        dtv = arena.get("dtv", [NH, TT], F32)
        ldt = dtv
        dtA = dtraw
        cbuf2 = [arena.get(f"cbuf{i}", [NH, Q], F32) for i in range(2)]
        rbuf2 = [arena.get(f"rbuf{i}", [NH, Q], F32) for i in range(2)]
        chi2 = [arena.get(f"chi{i}", [NH, Q], BF16) for i in range(2)]
        clo2 = [arena.get(f"clo{i}", [NH, Q], BF16) for i in range(2)]
        fb162 = [arena.get(f"fb16{i}", [NH, Q], BF16) for i in range(2)]
        fTs2 = [arena.get(f"fTs{i}", [128, NH], BF16) for i in range(2)]
        lrow2 = [arena.get(f"lrow{i}", [1, NH], BF16) for i in range(2)]
        scT2 = [arena.get(f"scT{i}", [128, Q], BF16) for i in range(2)]
        bTs2 = [arena.get(f"bTs{i}", [128, DST], BF16) for i in range(2)]
        cN2 = [arena.get(f"cN{i}", [64, Q], BF16) for i in range(2)]

        if o > 0:
            phase0(o)

        for l in range(NL):
            if o == 0 and l in (1, 2) and _quant_todo:
                quant_weights(*_quant_todo.pop(0))
            p.mmg([(pY.h[:, 0:NH], orow.h[0:1, :], dprow[l].h[:], True, True)],
                  [orow, dprow[l]], [pYL])
            for h in range(NH):
                p.ts("v", dpi.h[:, h * Q:(h + 1) * Q], ident.h[:],
                     pY.h[:, h:h + 1], None, A.mult, None, [ident, pYL], [dpi])
            for it in range(NI):
                itb = it * TT
                xap = lambda j: xseg.h[:, j * OT + itb: j * OT + itb + TT]
                fm_stats(xseg, xap, 8)
                rstd_mu(1024.0)
                bcast_rows(rrow, rrow.h[:], mrow, mrow.h[:])
                for h_ in range(2):
                    xv = svw(xseg.h, h_ * 4 * OT + itb, OT, 4, TT)
                    ov = (xn.h[:, h_ * 4 * TT:(h_ * 4 + 4) * TT]
                          .rearrange("p (n w) -> p n w", n=4))
                    p.tt("v", ov, xv, bcv(bcast.h[:, TT:2 * TT], 4), A.subtract,
                         [xseg, bcast], [xnH[h_]])
                    p.tt("v", ov, ov, bcv(bcast.h[:, 0:TT], 4), A.mult,
                         [xnH[h_], bcast], [xnH[h_]])
                    for j in range(h_ * 4, h_ * 4 + 4):
                        osl = xn.h[:, j * TT:(j + 1) * TT]
                        p.ts("v", osl, osl, lnw[l].h[:, j:j + 1], lnb[l].h[:, j:j + 1],
                             A.mult, A.add, [xnH[h_], lnw[l], lnb[l]], [xnH[h_]])
                p.cp("v",
                     convin.h[:, 0:XC * CIW].rearrange("p (j w) -> p j w", j=XC)[:, :, 0:3],
                     halo[l].h[:].rearrange("p (j w) -> p j w", j=XC),
                     [halo[l]], cvinJ)
                for k in range(8):
                    p.dma("g" if k < 4 else "y", wbig.h[:, k * 2208:(k + 1) * 2208],
                          inw_s[l, k * 128:(k + 1) * 128, 2048:EIN], [],
                          [wbA if k < 4 else wbB])
                for eg in (range(16, 22), range(22, 28), range(28, EC)):
                    for e in eg:
                        ew = 128 if e < 33 else 32
                        sa = slot_a[e % 6]
                        eo = (e - 16) * 128
                        mms = [(sa[0:ew, :] if ew < 128 else sa,
                                wbig.h[:, k * 2208 + eo: k * 2208 + eo + ew],
                                xn.h[:, k * TT:(k + 1) * TT], k == 0, False)
                               for k in range(4)]
                        p.mmg(mms, [wbA, xnH[0]], [slot_t[e % 6]])
                    for e in eg:
                        ew = 128 if e < 33 else 32
                        sl, sa = slot_t[e % 6], slot_a[e % 6]
                        eo = (e - 16) * 128
                        mms = [(sa[0:ew, :] if ew < 128 else sa,
                                wbig.h[:, k * 2208 + eo: k * 2208 + eo + ew],
                                xn.h[:, k * TT:(k + 1) * TT], False, k == 7)
                               for k in range(4, 8)]
                        p.mmg(mms, [wbB, xnH[1]], [sl])
                        if e < 33:
                            j = e - 16
                            p.cp("s", convin.h[:, j * CIW + 3: j * CIW + 3 + TT], sa,
                                 [sl], [cvinJ[j]])
                        else:
                            p.cp("s", dtraw.h, sa[0:NH, :], [sl], [dtraw])
                for k in range(8):
                    p.dma("g" if k < 4 else "y", wbig.h[:, k * 2208:k * 2208 + 2048],
                          inw_s[l, k * 128:(k + 1) * 128, 0:2048], [],
                          [wbA if k < 4 else wbB])
                for eg in (range(0, 6), range(6, 12), range(12, 16)):
                    for e in eg:
                        sa = slot_a[e % 6]
                        mms = [(sa, wbig.h[:, k * 2208 + e * 128:
                                           k * 2208 + (e + 1) * 128],
                                xn.h[:, k * TT:(k + 1) * TT], k == 0, False)
                               for k in range(4)]
                        p.mmg(mms, [wbA, xnH[0]], [slot_t[e % 6]])
                    for e in eg:
                        sl, sa = slot_t[e % 6], slot_a[e % 6]
                        mms = [(sa, wbig.h[:, k * 2208 + e * 128:
                                           k * 2208 + (e + 1) * 128],
                                xn.h[:, k * TT:(k + 1) * TT], False, k == 7)
                               for k in range(4, 8)]
                        p.mmg(mms, [wbB, xnH[1]], [sl])
                        p.act("s", zt.h[:, e * TT:(e + 1) * TT], sa, AF.Silu, [sl], [zt])
                for k in range(4):
                    wdep = [[wbA], [wbA], [wbA, wbB], [wbB]][k]
                    p.dma("g" if k % 2 == 0 else "y", wbig.h[:, k * 4096:(k + 1) * 4096]
                          .rearrange("p (a b) -> p a b", a=4),
                          outw_s[l, k * 512:(k + 1) * 512, :]
                          .rearrange("(a p) b -> p a b", p=128), [], wdep)
                for j in range(XC):
                    cb = j * CIW
                    p.ts("v", cvJ[j].h,
                         convin.h[:, cb: cb + TT], convw[l].h[:, j * 4:j * 4 + 1],
                         convbb[l].h[:, j:j + 1], A.mult, A.add,
                         [cvinJ[j], convw[l], convbb[l]], [cvJ[j]])
                for k in range(1, 4):
                    for j in range(XC):
                        cb = j * CIW
                        p.stt("v", cvJ[j].h,
                              convin.h[:, cb + k: cb + k + TT],
                              convw[l].h[:, j * 4 + k:j * 4 + k + 1],
                              cvJ[j].h, A.mult, A.add,
                              [cvinJ[j], convw[l], cvJ[j]], [cvJ[j]])
                p.cp("v", halo[l].h[:].rearrange("p (j w) -> p j w", j=XC),
                     convin.h[:, 0:XC * CIW].rearrange("p (j w) -> p j w", j=XC)[:, :, TT:TT + 3],
                     cvinJ, [halo[l]])
                cv3 = convout.h.rearrange("p (n s) -> p n s", n=XC)
                for qq in range(TT // Q):
                    cq = cv3[:, :, qq * Q:(qq + 1) * Q]
                    p.act("s", cq, cq, AF.Silu, cvJ, cvJ)
                p.act("s", dtv.h, dtraw.h, AF.Exp, [dtraw, dtb[l]], [dtv],
                      bias=dtb[l].h[:])
                p.act("s", dtv.h, dtv.h, AF.Ln, [dtv], [dtv], bias=1.0)
                p.ts("v", dtA.h, dtv.h, nA[l].h[:], None, A.mult, None,
                     [dtv, nA[l]], [dtA])
                p.act("s", ldt.h, dtv.h, AF.Ln, [dtv, dtA], [ldt])
                ldtc2 = [rg(ldtcat, f"ldtc{i}") for i in range(2)]
                selc2 = [rg(selcat, f"selc{i}") for i in range(2)]

                def cpath(q, pr):
                    qb = q * Q
                    cbuf, rbuf = cbuf2[pr], rbuf2[pr]
                    chi, clo, fb16 = chi2[pr], clo2[pr], fb162[pr]
                    fTs, lrow, scT, bTs, cN = (fTs2[pr], lrow2[pr], scT2[pr],
                                               bTs2[pr], cN2[pr])
                    lc = ldtc2[pr]
                    p.scan("v", cbuf.h, o32.h[:], dtA.h[:, qb:qb + Q], 0.0,
                           A.mult, A.add, [o32, dtA], [cbuf])
                    p.tt("v", rbuf.h, ldt.h[:, qb:qb + Q], cbuf.h, A.subtract,
                         [ldt, cbuf], [rbuf])
                    lcap = ldtcat.h[:, pr * Q:(pr + 1) * Q]
                    p.cp("v", lcap[0:NH, :], rbuf.h, [rbuf], [lc])
                    p.tt("v", lcap[NH:2 * NH, :], rbuf.h, lcap[0:NH, :],
                         A.subtract, [rbuf, lc], [lc])
                    p.cp("v", chi.h, cbuf.h, [cbuf], [chi])
                    p.tt("v", clo.h, cbuf.h, chi.h, A.subtract, [cbuf, chi], [clo])
                    bsl = convout.h[0:64, 16 * TT + qb: 16 * TT + qb + Q]
                    csl = convout.h[64:128, 16 * TT + qb: 16 * TT + qb + Q]
                    p.cp("v", cN.h, csl, [cvJ[16]], [cN])
                    p.mmg([(pD1.h[:, 0:Q], bsl, cN.h, True, True)],
                          [cvJ[16], cN], [pD1L])
                    p.tt("v", pD1.h[:, 0:Q], pD1.h[:, 0:Q], mask01.h[:], A.mult,
                         [pD1L, mask01], [pD1L])
                    p.cp("v", scT.h, pD1.h[:, 0:Q], [pD1L], [scT])
                    p.act("s", fb16.h, rbuf.h, AF.Exp, [rbuf, cbuf], [fb16],
                          bias=cbuf.h[:, Q - 1:Q])
                    p.trg([(pT.h[0:Q, 0:NH], fb16.h, ident.h[0:NH, 0:NH])],
                          [fb16, ident], [pT0])
                    p.cp("v", fTs.h[:, 0:NH], pT.h[0:Q, 0:NH], [pT0], [fTs])
                    p.act("s", sq.h[0:NH, 0:1], cbuf.h[:, Q - 1:Q], AF.Exp,
                          [cbuf], [sq])
                    p.trg([(pT.h[0:1, 512:512 + NH], sq.h[0:NH, 0:1],
                            ident.h[0:NH, 0:NH])], [sq, ident], [pT0])
                    p.cp("v", lrow.h, pT.h[0:1, 512:512 + NH], [pT0], [lrow])
                    p.trg([(pT.h[:, 1024:1024 + 64], bsl, ident.h[0:64, 0:64])],
                          [cvJ[16], ident], [pT1])
                    p.cp("v", bTs.h[:, 0:DST], pT.h[:, 1024:1024 + 64], [pT1], [bTs])

                def body(q, pr):
                    qb = q * Q
                    cbuf, rbuf = cbuf2[pr], rbuf2[pr]
                    fTs, lrow, scT, bTs, cN = (fTs2[pr], lrow2[pr], scT2[pr],
                                               bTs2[pr], cN2[pr])
                    p.dma("y", selcat.h[64:65, :], chi2[pr].h, [chi2[pr]], [selcat])
                    p.dma("y", selcat.h[65:66, :], clo2[pr].h, [clo2[pr]], [selcat])
                    lc = ldtc2[pr]
                    lcap = ldtcat.h[:, pr * Q:(pr + 1) * Q]
                    dsl = [pD0L, pD0R, pD1L, pD1R]
                    dsa = [pD0.h[:, 0:512], pD0.h[:, 512:1024],
                           pD1.h[:, 0:512], pD1.h[:, 512:1024]]
                    trs = [(pT.h[:, fc * 128:(fc + 1) * 128],
                            convout.h[:, fc * TT + qb: fc * TT + qb + Q], ident.h[:])
                           for fc in range(ZC)]
                    p.trg(trs, [cvJ[fc] for fc in range(ZC)] + [ident], [pT0, pT1])
                    p.cp("v", xsT.h, pT.h[:, 0:2048], [pT0, pT1], [xsT])
                    p.tt("v", xbar.h[:].rearrange("p (n w) -> p n w", n=NH),
                         xsT.h[:].rearrange("p (n w) -> p n w", n=NH),
                         fTs.h[:, 0:NH].unsqueeze(2).broadcast_to([128, NH, 64]),
                         A.mult, [xsT, fTs], [xbar])
                    dsl_ = [pD0L, pD0R, pD1L, pD1R]
                    for hf in range(2):
                        ho = hf * 16
                        for sub in range(4):
                            pc = hf * 4 + sub
                            p.mmg([(dsa[sub], lcap,
                                    selcat.h[:, pc * 512:(pc + 1) * 512], True, False),
                                   (dsa[sub], ident.h[:], mneg.h[:], False, True)],
                                  [lc, selcat, ident, mneg], [dsl[sub]])
                        p.act("s", stok.h[:, 0:1024], pD0.h[:, 0:1024], AF.Exp,
                              [dsl[0], dsl[1]], [stok])
                        p.act("s", stok.h[:, 1024:2048], pD1.h[:, 0:1024], AF.Exp,
                              [dsl[2], dsl[3]], [stok])
                        for sub in range(4):
                            pc = hf * 4 + sub
                            p.mmg([(dsa[sub][64:128, :], ldtcat.h[64:66, 0:64],
                                    selcat.h[64:66, pc * 512:(pc + 1) * 512],
                                    True, True)],
                                  [ldtcat, selcat], [dsl[sub]])
                        p.act("s", ssta.h[0:64, 0:1024], pD0.h[64:128, 0:1024],
                              AF.Exp, [dsl[0], dsl[1]], [ssta])
                        p.act("s", ssta.h[0:64, 1024:2048], pD1.h[64:128, 0:1024],
                              AF.Exp, [dsl[2], dsl[3]], [ssta])
                        p.stt("v", stok.h[:].rearrange("p (n w) -> p n w", n=16),
                              stok.h[:].rearrange("p (n w) -> p n w", n=16), CAP,
                              bcv(scT.h, 16), A.min, A.mult,
                              [stok, scT], [stok])
                        p.tt("v", stok.h[:], stok.h[:],
                             dpi.h[:, ho * Q:(ho + 16) * Q], A.add,
                             [stok, dpi], [stok])
                        p.tt("v", ssta.h[:].rearrange("p (n w) -> p n w", n=16),
                             ssta.h[:].rearrange("p (n w) -> p n w", n=16),
                             bcv(cN.h, 16), A.mult, [ssta, cN], [ssta])
                        mms = []
                        for pr_ in range(8):
                            for par in range(2):
                                hh = pr_ * 2 + par
                                h = ho + hh
                                tp = (0, 64) if par else None
                                obp = par * 64
                                oap = pY.h[obp:obp + 64, pr_ * Q:(pr_ + 1) * Q]
                                mms.append((oap, xsT.h[:, h * 64:(h + 1) * 64],
                                            stok.h[:, hh * Q:(hh + 1) * Q],
                                            True, False, tp))
                                mms.append((oap, hst[l].h[hro[l]:hro[l] + 64,
                                                          h * 64:(h + 1) * 64],
                                            ssta.h[:, hh * Q:(hh + 1) * Q], False,
                                            True, tp))
                        p.mmg(mms, [xsT, stok, hst[l], ssta],
                              [pYL, pYR])
                        zv = svw(zt.h, (hf * 8) * TT + qb, TT, 8, Q)
                        yv = svw(convin.h, (hf * 8) * CIW + 3 + qb, CIW, 8, Q)
                        p.tt("v", yv,
                             pY.h[:, 0:1024].rearrange("p (n w) -> p n w", n=8),
                             zv, A.mult, [pYL, pYR, zt],
                             [cvinJ[hf * 8 + i] for i in range(8)])
                        if hf == 0 and q + 1 < NQ:
                            cpath(q + 1, (q + 1) % 2)
                    lam_v = [lrow.h[0:1, g * 8:(g + 1) * 8]
                             .unsqueeze(2).broadcast_to([1, 8, 64]) for g in range(4)]
                    p.mmg([(pY.h[0:64, 0:512], orow.h[0:1, 0:64],
                            lam_v[0], True, True)], [orow, lrow], [pYL])
                    p.mmg([(pY.h[0:64, 512:1024], orow.h[0:1, 0:64],
                            lam_v[1], True, True)], [orow, lrow], [pYR])
                    p.mmg([(pD0.h[0:64, 0:512], orow.h[0:1, 0:64],
                            lam_v[2], True, True)], [orow, lrow], [pD0L])
                    p.mmg([(pD0.h[0:64, 512:1024], orow.h[0:1, 0:64],
                            lam_v[3], True, True)], [orow, lrow], [pD0R])
                    p.tt("v", hst[l].h[hro[l]:hro[l] + 64, 0:1024],
                         hst[l].h[hro[l]:hro[l] + 64, 0:1024],
                         pY.h[0:64, 0:1024], A.mult, [hst[l], pYL, pYR],
                         [hst[l]])
                    p.tt("v", hst[l].h[hro[l]:hro[l] + 64, 1024:2048],
                         hst[l].h[hro[l]:hro[l] + 64, 1024:2048],
                         pD0.h[0:64, 0:1024], A.mult, [hst[l], pD0L, pD0R],
                         [hst[l]])
                    p.mmg([(pY.h[0:64, 0:512], bTs.h[:, 0:DST],
                            xbar.h[:, 0:512], True, True)], [bTs, xbar], [pYL])
                    p.mmg([(pY.h[0:64, 512:1024], bTs.h[:, 0:DST],
                            xbar.h[:, 512:1024], True, True)], [bTs, xbar], [pYR])
                    p.mmg([(pD0.h[0:64, 0:512], bTs.h[:, 0:DST],
                            xbar.h[:, 1024:1536], True, True)], [bTs, xbar], [pD0L])
                    p.mmg([(pD0.h[0:64, 512:1024], bTs.h[:, 0:DST],
                            xbar.h[:, 1536:2048], True, True)], [bTs, xbar], [pD0R])
                    p.tt("v", hst[l].h[hro[l]:hro[l] + 64, 0:1024],
                         hst[l].h[hro[l]:hro[l] + 64, 0:1024],
                         pY.h[0:64, 0:1024], A.add, [hst[l], pYL, pYR],
                         [hst[l]])
                    p.tt("v", hst[l].h[hro[l]:hro[l] + 64, 1024:2048],
                         hst[l].h[hro[l]:hro[l] + 64, 1024:2048],
                         pD0.h[0:64, 0:1024], A.add, [hst[l], pD0L, pD0R],
                         [hst[l]])

                cpath(0, 0)
                for q in range(NQ):
                    body(q, q % 2)
                yap = lambda j: convin.h[:, j * CIW + 3: j * CIW + 3 + TT]
                for j in range(ZC):
                    b = (sq, sq2)[j % 2]
                    p.act("s", b.h[:, 0:TT], yap(j), AF.Square, [cvinJ[j]], [b])
                    p.mmg([(pY.h[0:1, 0:TT], ocol.h[:], b.h[:, 0:TT],
                            j == 0, j == ZC - 1)], [b, ocol], [pYL])
                p.ts("v", srow.h[:], pY.h[0:1, 0:TT], 1.0 / DIN, None, A.mult, None,
                     [pYL], [srow])
                p.act("s", rrow.h[:], srow.h[:], AF.Ln, [srow], [rrow], bias=EPS)
                p.act("s", rrow.h[:], rrow.h[:], AF.Exp, [rrow], [rrow], scale=-0.5)
                p.cp("v", brow.h[0:1, 0:TT], rrow.h[:], [rrow], [brow])
                p.mmg([(pD1.h[:, 0:512], orow.h[0:1, :], brow.h[0:1, 0:TT],
                        True, True)], [brow, orow], [pD1L])
                p.cp("v", bcast.h[:, 0:TT], pD1.h[:, 0:512], [pD1L], [bcast])
                yv_all = svw(convin.h, 3, CIW, ZC, TT)
                p.tt("v", yv_all, yv_all, bcv(bcast.h[:, 0:TT], ZC), A.mult,
                     cvinJ[:ZC] + [bcast], cvinJ[:ZC])
                for j in range(ZC):
                    p.ts("v", yap(j), yap(j), nwv[l].h[:, j:j + 1], None, A.mult,
                         None, [cvinJ[j], nwv[l]], [cvinJ[j]])
                for dg in range(2):
                    for k in range(16):
                        mms = [(slot_a[di], wbig.h[:, k * 1024 + (dg * 4 + di) * 128:
                                                   k * 1024 + (dg * 4 + di + 1) * 128],
                                yap(k), k == 0, k == 15) for di in range(4)]
                        rdep = [wbA] if k < 8 else ([wbA, wbB] if k == 8 else [wbB])
                        p.mmg(mms, rdep + [cvinJ[k]],
                              [slot_t[0], slot_t[1], slot_t[2], slot_t[3]])
                    for di in range(4):
                        d = dg * 4 + di
                        xa = xseg.h[:, d * OT + itb: d * OT + itb + TT]
                        p.tt("v", xa, slot_a[di], xa, A.add,
                             [slot_t[di], xseg], [xseg])

        for tti in range(OT // 128):
            tb = tti * 128
            p.dma("y", stage.h[:], x_h[ob + tb: ob + tb + 128, :], [], [stage])
            p.cp("v", stg16.h[:], stage.h[:], [stage], [stg16])
            trs = [(pT.h[:, j * 128:(j + 1) * 128], stg16.h[:, j * 128:(j + 1) * 128],
                    ident.h[:]) for j in range(8)]
            p.trg(trs, [stg16, ident], [pT0, pT1])
            xa = svw(xseg.h, tb, OT, 8, 128)
            p.tt("v", xa, xa, pT.h[:, 0:1024].rearrange("p (n w) -> p n w", n=8),
                 A.add, [xseg, pT0, pT1], [xseg])

        arena.newgen()
        fA = arena.get("fA", [128, 8 * TT], BF16)
        fB = arena.get("fB", [128, 8 * TT], BF16)
        gl = arena.get("gl", [128, 32 * TT], BF16)
        fC = arena.get("fC", [128, 32 * TT], BF16)
        iqt = arena.get("iqt", [128, 4 * TT], I32)
        fq = arena.get("fq", [128, 4 * TT], F32)
        sqm = arena.get("sqm", [128, 2 * TT], BF16)
        _sqm_holder["t"] = sqm

        for it in range(NI):
            itb = it * TT
            xap = lambda j: xseg.h[:, j * OT + itb: j * OT + itb + TT]
            fap = lambda j: fA.h[:, j * TT:(j + 1) * TT]
            bap = lambda j: fB.h[:, j * TT:(j + 1) * TT]
            gap = lambda j: gl.h[:, j * TT:(j + 1) * TT]
            cap = lambda j: fC.h[:, j * TT:(j + 1) * TT]
            fm_stats(xseg, xap, 8)
            rstd_mu(1024.0)
            bcast_rows(rrow, rrow.h[:], mrow, mrow.h[:])
            xv = svw(xseg.h, itb, OT, 8, TT)
            av = fA.h[:].rearrange("p (n w) -> p n w", n=8)
            p.tt("v", av, xv, bcv(bcast.h[:, TT:2 * TT], 8), A.subtract,
                 [xseg, bcast], [fA])
            p.tt("v", av, av, bcv(bcast.h[:, 0:TT], 8), A.mult, [fA, bcast], [fA])
            fm_stats(fA, fap, 8)
            rstd_mu(1024.0)
            bcast_rows(rrow, rrow.h[:], mrow, mrow.h[:])
            bv = fB.h[:].rearrange("p (n w) -> p n w", n=8)
            p.tt("v", bv, av, bcv(bcast.h[:, TT:2 * TT], 8), A.subtract,
                 [fA, bcast], [fB])
            p.tt("v", bv, bv, bcv(bcast.h[:, 0:TT], 8), A.mult, [fB, bcast], [fB])
            for j in range(8):
                p.ts("v", bap(j), bap(j), g1c.h[:, j:j + 1], e1c.h[:, j:j + 1],
                     A.mult, A.add, [fB, g1c, e1c], [fB])
            qchain(fB, 8, wiv[0], wiv128[0])
            for hw in range(2):
                hco = hw * 16
                for k in range(8):
                    p.dma("g" if k < 4 else "y", wbig.h[:, k * 2048:(k + 1) * 2048],
                          qw1_s[k * 128:(k + 1) * 128, hw * 2048:(hw + 1) * 2048],
                          [], [wbA, wbB])
                for hc0 in range(16):
                    hc = hco + hc0
                    sl, sa = slot_t[hc % 6], slot_a[hc % 6]
                    mms = [(sa, wbig.h[:, k * 2048 + hc0 * 128: k * 2048 + (hc0 + 1) * 128],
                            bap(k), k == 0, k == 7) for k in range(8)]
                    p.mmg(mms, [wbA, wbB, fB], [sl])
                    p.tt("v", stage.h[:, 0:TT], sa, bcast.h[:, TT:2 * TT], A.mult,
                         [sl, bcast], [stage])
                    p.act("s", gap(hc), stage.h[:, 0:TT], AF.Gelu, [stage, b1c], [gl],
                          bias=b1c.h[:, hc:hc + 1])
            fm_stats(gl, gap, 32)
            rstd_mu(4096.0)
            bcast_rows(rrow, rrow.h[:], mrow, mrow.h[:])
            gv = gl.h[:].rearrange("p (n w) -> p n w", n=32)
            cv = fC.h[:].rearrange("p (n w) -> p n w", n=32)
            p.tt("v", cv, gv, bcv(bcast.h[:, TT:2 * TT], 32), A.subtract,
                 [gl, bcast], [fC])
            p.tt("v", cv, cv, bcv(bcast.h[:, 0:TT], 32), A.mult, [fC, bcast], [fC])
            for j in range(32):
                p.ts("v", cap(j), cap(j), g2c.h[:, j:j + 1], e2c.h[:, j:j + 1],
                     A.mult, A.add, [fC, g2c, e2c], [fC])
            qchain(fC, 32, wiv[1], wiv128[1])
            p.cp("v", invc.h[:], mq4.h[:], [mq4], [invc])
            for dh in range(2):
                for k in range(8):
                    p.dma("g" if k < 4 else "y",
                          wbig.h[:, k * 2048:(k + 1) * 2048].rearrange("p (a b) -> p a b", a=4),
                          qw2_s[k * 512:(k + 1) * 512, dh * 512:(dh + 1) * 512]
                          .rearrange("(a p) b -> p a b", p=128), [], [wbA, wbB])
                for tq in range(4):
                    tqb = itb + tq * 128
                    trs = [(pT.h[:, 1024 + j * 128: 1024 + (j + 1) * 128],
                            xseg.h[:, (dh * 4 + j) * OT + tqb: (dh * 4 + j) * OT + tqb + 128],
                            ident.h[:]) for j in range(4)]
                    p.trg(trs, [xseg, ident], [pT1])
                    sl, sa = slot_t[2 + (tq % 2)], slot_a[2 + (tq % 2)]
                    mms = []
                    for k in range(32):
                        lhs = fC.h[:, k * TT + tq * 128: k * TT + tq * 128 + 128]
                        rhs = wbig.h[:, (k // 4) * 2048 + (k % 4) * 512:
                                     (k // 4) * 2048 + (k % 4) * 512 + 512]
                        mms.append((sa, lhs, rhs, k == 0, k == 31))
                    p.mmg(mms, [fC, wbA, wbB], [sl])
                    so = stage.h[:, (tq % 2) * 512:((tq % 2) + 1) * 512]
                    p.ts("v", so, sa, invc.h[:, tq:tq + 1], None, A.mult, None,
                         [sl, invc], [stage])
                    p.tt("v", so, so, b2bc.h[:, dh * 512:(dh + 1) * 512], A.add,
                         [stage, b2bc], [stage])
                    p.tt("v", so, so, pT.h[:, 1024:1024 + 512], A.add,
                         [stage, pT1], [stage])
                    tok = p.dma("y", out_h[ob + tqb: ob + tqb + 128,
                                           dh * 512:(dh + 1) * 512], so,
                                [stage], [NONE])
                    store_toks.append(tok)

    for tok in store_toks:
        p._wait("y", tok)

    p.es.close()
    return nc


def _sel_const():
    sel = np.zeros((64, NH * Q), np.float32)
    for h in range(NH):
        sel[h, h * Q:(h + 1) * Q] = 1.0
        sel[NH + h, h * Q:(h + 1) * Q] = 1.0
    return sel.astype(ml_dtypes.bfloat16)


def kernel(**inputs):
    ins = {k: np.asarray(v) for k, v in inputs.items()}
    x = np.ascontiguousarray(ins["L0_attn_x"], dtype=np.float32)
    def _pp(a, nc_):
        a = np.asarray(a, np.float32)
        return np.ascontiguousarray(np.moveaxis(a.reshape(*a.shape[:-1], nc_, 128), -1, -2))
    conv_w = np.ascontiguousarray(
        np.transpose(ins["L0_attn_conv_w"], (0, 2, 1)).reshape(NL, XC, 128, 4)
        .transpose(0, 2, 1, 3).reshape(NL, 128, XC * 4), dtype=np.float32)
    common = {
        "in_w": np.ascontiguousarray(ins["L0_attn_in_w"], np.float32),
        "out_w": np.ascontiguousarray(ins["L0_attn_out_w"], np.float32),
        "conv_w": conv_w,
        "conv_b": _pp(ins["L0_attn_conv_b"], XC),
        "ln_w": _pp(ins["L0_attn_ln_w"], 8),
        "ln_b": _pp(ins["L0_attn_ln_b"], 8),
        "dt_bias": np.ascontiguousarray(ins["L0_attn_dt_bias"], np.float32),
        "A_log": np.ascontiguousarray(ins["L0_attn_A_log"], np.float32),
        "Dp": np.ascontiguousarray(ins["L0_attn_Dp"], np.float32).reshape(NL, 1, NH),
        "norm_w": _pp(ins["L0_attn_norm_w"], ZC),
        "w1": np.ascontiguousarray(ins["L0_ffn_w1"], np.float32),
        "b1": _pp(ins["L0_ffn_b1"], 32),
        "w2": np.ascontiguousarray(ins["L0_ffn_w2"], np.float32),
        "b2": np.ascontiguousarray(ins["L0_ffn_b2"], np.float32).reshape(1, D),
        "ln1_w": _pp(ins["L0_ffn_ln1_w"], 8),
        "ln1_b": _pp(ins["L0_ffn_ln1_b"], 8),
        "ln2_w": _pp(ins["L0_ffn_ln2_w"], 32),
        "ln2_b": _pp(ins["L0_ffn_ln2_b"], 32),
        "c_ident": np.eye(128, dtype=np.float32).astype(ml_dtypes.bfloat16),
        "c_mask": np.triu(np.ones((128, 128), np.float32)).astype(ml_dtypes.bfloat16),
        "c_sel": _sel_const(),
        "c_mneg": np.tile(np.where(np.triu(np.ones((128, 128), np.float32)) > 0, 0.0, -1e9).astype(np.float32), (1, 4)).astype(ml_dtypes.bfloat16),
        "c_orow": np.ones((1, 128), np.float32).astype(ml_dtypes.bfloat16),
        "c_ocol": np.ones((128, 1), np.float32).astype(ml_dtypes.bfloat16),
        "c_oc32": np.ones((128, 1), np.float32),
        "c_or32": np.ones((1, 128), np.float32),
        "c_o32": np.ones((NH, Q), np.float32),
    }
    nc = build()
    in_maps = [dict(common, x=np.ascontiguousarray(x[b])) for b in range(B)]
    res = run_bass_kernel_spmd(nc, in_maps, core_ids=list(range(8)))
    out = np.stack([res.results[b]["out"] for b in range(B)], axis=0)
    return out.astype(np.float32)
